# version 28
# baseline (speedup 1.0000x reference)
import numpy as np
import ml_dtypes
import concourse.bass as bass
import concourse.mybir as mybir
from concourse.bass_utils import run_bass_kernel_spmd
from contextlib import ExitStack

F32 = mybir.dt.float32
BF16 = mybir.dt.bfloat16
AF = mybir.ActivationFunctionType
ALU = mybir.AluOpType
AX = mybir.AxisListType

S, D, H, HD, HP = 4096, 1024, 16, 64, 8
DF, RK = 4096, 256
NBLK, CB, NJT = 8, 512, 32
EPS = 1e-5
SCALE = HD ** -0.5
NEWTON = {"ln1": 9, "ln2": 11, "ln3": 9, "ln4": 11}

ENGINES = ("PE", "ACT", "DVE", "POOL", "SP")


class Sched:

    def __init__(self):
        self.ops = []
        self.last_write = {}
        self.readers = {}

    def add(self, engine, fn, reads=(), writes=(), dma=False, name=""):
        idx = len(self.ops)
        waits = set()
        for k in reads:
            w = self.last_write.get(k)
            if w is not None:
                wo = self.ops[w]
                if wo["dma"] or wo["engine"] != engine or engine != "PE":
                    waits.add(w)
            if isinstance(k, tuple) and k and k[0] == "ps":
                for r in self.readers.get(k, ()):
                    if self.ops[r]["engine"] != engine:
                        waits.add(r)
        for k in writes:
            w = self.last_write.get(k)
            if w is not None:
                wo = self.ops[w]
                if wo["dma"] or dma or wo["engine"] != engine or engine != "PE":
                    waits.add(w)
            for r in self.readers.get(k, ()):
                ro = self.ops[r]
                if ro["dma"] or dma or ro["engine"] != engine or engine != "PE":
                    waits.add(r)
        for k in reads:
            self.readers.setdefault(k, []).append(idx)
        for k in writes:
            self.last_write[k] = idx
            self.readers[k] = []
        self.ops.append(dict(engine=engine, fn=fn, waits=sorted(waits),
                             dma=dma, name=name, needs_event=False, event=None))
        return idx

    def finalize(self):
        for op in self.ops:
            for w in op["waits"]:
                self.ops[w]["needs_event"] = True
        tick = {e: 0 for e in ENGINES}
        dmac = {e: 0 for e in ENGINES}
        for op in self.ops:
            if op["dma"]:
                dmac[op["engine"]] += 1
                op["event"] = ("dma", op["engine"], dmac[op["engine"]] * 16)
            elif op["needs_event"]:
                tick[op["engine"]] += 1
                op["event"] = ("tick", op["engine"], tick[op["engine"]])
        self.dma_totals = dmac

    def emit(self, engine, eng, sems):
        hw = {}
        for op in self.ops:
            if op["engine"] != engine:
                continue
            for w in op["waits"]:
                ev = self.ops[w]["event"]
                kind, e2, val = ev
                k2 = (kind, e2)
                if hw.get(k2, 0) >= val:
                    continue
                eng.wait_ge(sems[k2], val)
                hw[k2] = val
            inst = op["fn"](eng)
            ev = op["event"]
            if ev is not None:
                kind, e2, val = ev
                if kind == "dma":
                    inst.then_inc(sems[("dma", e2)], 16)
                else:
                    inst.then_inc(sems[("tick", e2)], 1)


def host_consts():
    pos = np.arange(S, dtype=np.float64)
    invf = 10000.0 ** (-(np.arange(0, HD, 2, dtype=np.float64) / HD))
    a = (np.arange(128) % 64) // 2
    ang = pos[None, :] * invf[a][:, None]
    cos_t = np.cos(ang).astype(ml_dtypes.bfloat16)
    sin_t = np.sin(ang).astype(ml_dtypes.bfloat16)
    rot = np.zeros((128, 128), np.float32)
    for i in range(0, 128, 2):
        rot[i + 1, i] = -1.0
        rot[i, i + 1] = 1.0
    rot = rot.astype(ml_dtypes.bfloat16)
    ident = np.eye(128, dtype=ml_dtypes.bfloat16)
    masks = np.zeros((128, 4, 512), np.float32)
    for o in range(4):
        for j in range(128):
            masks[j, o, o * 128 + j:] = 1.0
    masks = masks.astype(ml_dtypes.bfloat16)
    ones_c = np.ones((128, 1), dtype=ml_dtypes.bfloat16)
    ones_r64 = np.ones((1, 64), dtype=ml_dtypes.bfloat16)
    ones_r128f = np.ones((1, 128), np.float32)
    ones_r128b = np.ones((1, 128), dtype=ml_dtypes.bfloat16)
    return dict(cos_t=cos_t, sin_t=sin_t, rot=rot, ident=ident, masks=masks,
                ones_c=ones_c, ones_r64=ones_r64, ones_r128f=ones_r128f,
                ones_r128b=ones_r128b)


def build(nc, es, stage=3, dbg=(), nattn=NBLK, parts=frozenset({'mm1','exp','mask','mm2','oevac','proj','projb','ln2'}), barriers=()):
    def din(name, shape, dt=F32):
        return nc.dram_tensor(name, list(shape), dt, kind="ExternalInput").ap()

    x_d = din("x", (S, D))
    wqT_d = din("wqT", (D, 3 * D))
    qkvbT_d = din("qkvbT", (128, 24))
    pwT_d = din("pwT", (D, D))
    pb_d = din("pb", (1, D))
    u1_d = din("u1", (D, RK))
    v1_d = din("v1", (RK, DF))
    b1T_d = din("b1T", (128, 32))
    u2_d = din("u2", (DF, RK))
    v2_d = din("v2", (RK, D))
    b2_d = din("b2", (1, D))
    cos_d = din("cos_t", (128, S), BF16)
    sin_d = din("sin_t", (128, S), BF16)
    rot_d = din("rot", (128, 128), BF16)
    ident_d = din("ident", (128, 128), BF16)
    masks_d = din("masks", (128, 4, 512), BF16)
    onesc_d = din("ones_c", (128, 1), BF16)
    onesr64_d = din("ones_r64", (1, 64), BF16)
    onesr128f_d = din("ones_r128f", (1, 128), F32)
    onesr128b_d = din("ones_r128b", (1, 128), BF16)

    out_d = nc.dram_tensor("out", [S, D], F32, kind="ExternalOutput").ap()

    k_dram = nc.dram_tensor("k_scr", [128, HP, NJT, 128], BF16, kind="Internal").ap()
    v_dram = nc.dram_tensor("v_scr", [128, NJT, H, 65], BF16, kind="Internal").ap()
    x2_dram = nc.dram_tensor("x2_scr", [S, D], F32, kind="Internal").ap()

    cur = [16512]
    total_budget = 229376

    def sbuf(name, shape, dt, at=None):
        nbytes = int(np.prod(shape[1:])) * (2 if dt == BF16 else 4)
        nbytes = (nbytes + 31) // 32 * 32
        if at is None:
            off = cur[0]
            cur[0] += nbytes
        else:
            off = at
        assert off + nbytes <= total_budget, f"sbuf overflow at {name}: {off+nbytes}"
        return nc.alloc_sbuf_tensor_at(name, list(shape), dt, offset=off).ap()

    qres = sbuf("qres", (128, HP, S), BF16)
    OT = sbuf("OT", (128, HP, 512), BF16)
    pb0_off = cur[0]
    pbuf0 = sbuf("pbuf0", (128, 2048), BF16)
    pbuf1 = sbuf("pbuf1", (128, 1024), BF16)
    xlnTB = sbuf("xlnTB", (128, 8, 512), BF16, at=pb0_off)
    pmb = sbuf("pmb", (128, 2, 1024), BF16)
    kj = sbuf("kj", (128, 3, 4, 128), BF16)
    vj = sbuf("vj", (128, 3, 4, 2, 65), BF16)
    pool = sbuf("pool", (128, 3, D), F32)
    x2o = sbuf("x2o", (128, 2, D), F32)
    sbrec = sbuf("sbrec", (128, 512), BF16)
    cosb = sbuf("cosb", (128, 1, 512), BF16)
    sinb = sbuf("sinb", (128, 1, 512), BF16)
    masks_sb = sbuf("masks_sb", (128, 4, 512), BF16)
    rot_sb = sbuf("rot_sb", (128, 128), BF16)
    ident_sb = sbuf("ident_sb", (128, 128), BF16)
    onesc_sb = sbuf("onesc_sb", (128, 1), BF16)
    onesr64_sb = sbuf("onesr64_sb", (1, 64), BF16)
    onesr128f_sb = sbuf("onesr128f_sb", (1, 128), F32)
    onesr128b_sb = sbuf("onesr128b_sb", (1, 128), BF16)
    qkvbT_sb = sbuf("qkvbT_sb", (128, 24), F32)
    b1T_sb = sbuf("b1T_sb", (128, 32), F32)
    pb_sb = sbuf("pb_sb", (1, D), BF16)
    b2_sb = sbuf("b2_sb", (1, D), BF16)
    negc = sbuf("negc", (128, 1), F32)
    recb = sbuf("recb", (1, 512), BF16)
    nsum = sbuf("nsum", (128, 4), F32)
    sumsq = sbuf("sumsq", (128, 4), F32)
    msq = sbuf("msq", (128, 4), F32)
    veps = sbuf("veps", (128, 4), F32)
    rst = sbuf("rst", (128, 4), F32)
    lbias = sbuf("lbias", (128, 4), F32)
    rs1 = sbuf("rs1", (128, 4), F32)
    rs2 = sbuf("rs2", (128, 4), F32)
    qmax = sbuf("qmax", (1, 64), F32)
    kmax = sbuf("kmax", (1, 64), F32)
    cval = sbuf("cval", (1, 4), F32)
    devnull = sbuf("devnull", (128, 1024), BF16)
    xln = sbuf("xln", (128, 2, D), BF16)
    xlnT = sbuf("xlnT", (128, 8, 512), BF16)
    afft = sbuf("afft", (128, D), F32)

    ov_base = cur[0]
    wqT = sbuf("wqT_sb", (128, 8, 3 * D), BF16, at=ov_base)
    qkf = sbuf("qkf", (128, 2, 512), BF16, at=ov_base + 49152)
    tsin = sbuf("tsin", (128, 512), BF16, at=ov_base + 51200)
    tcos = sbuf("tcos", (128, 512), BF16, at=ov_base + 52224)
    sqb = sbuf("sqb", (128, 2, 512), BF16, at=ov_base + 53248)
    kstage = sbuf("kstage", (128, 2, 512), BF16, at=ov_base + 55296)
    vstage = sbuf("vstage", (128, 2, 4, 2, 65), BF16, at=ov_base + 57344)
    p1_end = ov_base + 57344 + 2 * 4 * 2 * 65 * 2
    pwT = sbuf("pwT_sb", (128, 8, D), BF16, at=ov_base)
    u1 = sbuf("u1_sb", (128, 8, RK), BF16, at=ov_base + 16384)
    v1 = sbuf("v1_sb", (128, 2, DF), BF16, at=ov_base + 20480)
    u2 = sbuf("u2_sb", (128, 32, RK), BF16, at=ov_base + 36864)
    v2 = sbuf("v2_sb", (128, 2, D), BF16, at=ov_base + 53248)
    ptb = sbuf("ptb", (128, 2, 512), BF16, at=ov_base + 57344)
    htb = sbuf("htb", (128, 2, 512), BF16, at=ov_base + 59392)
    q2b = sbuf("q2b", (128, 4, 256), BF16, at=ov_base + 61440)
    q2t = sbuf("q2t", (128, 2, 4, 128), BF16, at=ov_base + 63488)
    a_end = ov_base + 63488 + 2048
    sbrec2 = sbuf("sbrec2", (1, 2, 512), F32, at=ov_base + 59424)
    cur[0] = max(p1_end, a_end)
    assert cur[0] <= total_budget, f"total sbuf {cur[0]}"

    sAB = es.enter_context(nc.psum_tensor("sAB", [128, 2048], F32)).ap()
    o4 = es.enter_context(nc.psum_tensor("o4", [128, 2048], F32)).ap()

    def psA(bank):
        return sAB[:, bank * 512:(bank + 1) * 512]

    def psO(bank):
        return o4[:, bank * 512:(bank + 1) * 512]

    sAB_bf = sAB.bitcast(BF16)
    o4_bf = o4.bitcast(BF16)

    XL = [xlnT, xlnTB]

    sch = Sched()
    OV = "OVP1"
    barn = [0]

    def barrier(tag):
        barn[0] += 1
        bid = barn[0]
        for e in ENGINES:
            sch.add(e, lambda x: x.nop(), reads=(), writes=[("bar1", bid, e)],
                    name=f"bar1_{tag}_{e}")
        for e in ENGINES:
            sch.add(e, lambda x: x.nop(),
                    reads=[("bar1", bid, e2) for e2 in ENGINES],
                    writes=(), name=f"bar2_{tag}_{e}")

    def lp(f):
        def g(e):
            with nc.allow_low_precision(reason="softmax 1/sum broadcast in bf16; "
                                        "same rounding class as bf16 matmul operands"):
                return f(e)
        return g

    def ld(engine, dst, src, name, writes, reads=()):
        sch.add(engine, lambda e, d=dst, s=src: e.dma_start(out=d, in_=s),
                reads=reads, writes=writes, dma=True, name=name)

    ld("SP", masks_sb[:], masks_d[:], "masks", [("masks",)])
    ld("SP", rot_sb[:], rot_d[:], "rot", [("rot",)])
    ld("SP", ident_sb[:], ident_d[:], "ident", [("ident",)])
    ld("SP", onesc_sb[:], onesc_d[:], "onesc", [("onesc",)])
    ld("SP", onesr64_sb[:], onesr64_d[:], "onesr64", [("onesr64",)])
    ld("SP", onesr128f_sb[:], onesr128f_d[:], "onesr128f", [("onesr128f",)])
    ld("SP", onesr128b_sb[:], onesr128b_d[:], "onesr128b", [("onesr128b",)])
    ld("SP", qkvbT_sb[:], qkvbT_d[:], "qkvbT", [("qkvbT",)])
    ld("SP", b1T_sb[:], b1T_d[:], "b1T", [("b1T",)])
    for dc in range(8):
        wk = [("wqT", dc), OV] + ([("wq_fence",)] if dc == 7 else [])
        ld("POOL", wqT[:, dc, :], wqT_d[dc * 128:(dc + 1) * 128, :],
           f"wqT{dc}", wk)
    sch.add("DVE", lambda e: e.memset(pbuf0[:], 0.0), reads=(), writes=[("pbz0",)], name="pbz0")
    sch.add("DVE", lambda e: e.memset(pbuf1[:], 0.0), reads=[("pbz0",)], writes=[("pb", 0), ("pb", 1), ("pb", 2)], name="pbz1")
    sch.add("DVE", lambda e: e.memset(vstage[:, :, :, :, 64:65], 1.0),
            reads=[OV], writes=[("vones",)], name="vones")

    def newton_rstd(src_key, n_iter, wslice):
        V = veps[:, wslice]
        sch.add("ACT", lambda e: e.activation(rs1[:, wslice], V, AF.Ln),
                reads=[src_key], writes=[("rs1",)], name="rstd_ln")
        sch.add("ACT", lambda e: e.activation(rst[:, wslice], rs1[:, wslice], AF.Exp, scale=-0.5),
                reads=[("rs1",)], writes=[("rstd",)], name="rstd_exp")

    def ln_stats_tile(xap, tslot, xkey):
        sch.add("DVE", lambda e: e.reduce_sum(nsum[:, tslot:tslot + 1], xap, axis=AX.X, negate=True),
                reads=[xkey], writes=[("nsum",)], name="ln_ns")
        sch.add("ACT", lambda e: e.activation(devnull[:], xap, AF.Square,
                                              accum_out=sumsq[:, tslot:tslot + 1]),
                reads=[xkey], writes=[("sumsq",), ("devnull",)], name="ln_sq")

    def ln_chunk_rstd(n_iter, nt=4):
        sl = slice(0, nt)
        sch.add("DVE", lambda e: e.tensor_single_scalar(msq[:, sl], nsum[:, sl], 1.0 / D, ALU.mult),
                reads=[("nsum",)], writes=[("msq",)])
        sch.add("DVE", lambda e: e.tensor_tensor(rs1[:, sl], msq[:, sl], msq[:, sl], ALU.mult),
                reads=[("msq",)], writes=[("rs1",)])
        sch.add("DVE", lambda e: e.scalar_tensor_tensor(veps[:, sl], sumsq[:, sl], 1.0 / D,
                                                        rs1[:, sl], ALU.mult, ALU.subtract),
                reads=[("sumsq",), ("rs1",)], writes=[("veps",)])
        sch.add("DVE", lambda e: e.tensor_single_scalar(veps[:, sl], veps[:, sl], EPS, ALU.add),
                reads=[("veps",)], writes=[("veps",)])
        newton_rstd(("veps",), n_iter, sl)
        sch.add("DVE", lambda e: e.tensor_tensor(lbias[:, sl], msq[:, sl], rst[:, sl], ALU.mult),
                reads=[("msq",), ("rstd",)], writes=[("lbias",)])

    for c in range(NBLK):
        cs = slice(c * CB, (c + 1) * CB)
        cpar = 0
        ld("SP", cosb[:, cpar, :], cos_d[:, cs], f"cos{c}", [("cosb", cpar), OV])
        ld("SP", sinb[:, cpar, :], sin_d[:, cs], f"sin{c}", [("sinb", cpar), OV])
        for pair in range(2):
            for t in (2 * pair, 2 * pair + 1):
                gt = c * 4 + t
                ld("SP", pool[:, t % 3, :], x_d[gt * 128:(gt + 1) * 128, :], f"x{gt}", [("pool", t % 3)])
            for t in (2 * pair, 2 * pair + 1):
                ln_stats_tile(pool[:, t % 3, :], t % 2, ("pool", t % 3))
            ln_chunk_rstd(NEWTON["ln1"], nt=2)
            for t in (2 * pair, 2 * pair + 1):
                tp = t % 2
                sch.add("DVE", lambda e, t=t, tp=tp: e.tensor_scalar(
                    xln[:, tp, :], pool[:, t % 3, :], rst[:, tp:tp + 1], lbias[:, tp:tp + 1], ALU.mult, ALU.add),
                    reads=[("pool", t % 3), ("rstd",), ("lbias",)], writes=[("xln", tp)], name="ln1app")
                for dc in range(8):
                    bpar = dc % 2
                    sch.add("PE", lambda e, tp=tp, dc=dc, bpar=bpar: e.transpose(
                        o4_bf[:, (2 + bpar) * 1024:(2 + bpar) * 1024 + 128],
                        xln[:, tp, dc * 128:(dc + 1) * 128], ident_sb[:]),
                        reads=[("xln", tp), ("ident",)], writes=[("ps", 6 + bpar)], name="xlt")
                    sch.add("DVE", lambda e, t=t, dc=dc, bpar=bpar, c=c: e.tensor_copy(
                        XL[c % 2][:, dc, t * 128:(t + 1) * 128],
                        o4_bf[:, (2 + bpar) * 1024:(2 + bpar) * 1024 + 128]),
                        reads=[("ps", 6 + bpar)], writes=[("xlnT", c % 2, dc)], name="xltc")
        def stageA(oc):
            bk = oc % 2
            for dc in range(8):
                sch.add("PE", lambda e, oc=oc, dc=dc, bk=bk, c=c: e.matmul(
                    psA(bk), wqT[:, dc, oc * 128:(oc + 1) * 128], XL[c % 2][:, dc, :],
                    start=(dc == 0), stop=(dc == 7)),
                    reads=[("wqT", dc), ("wq_fence",), ("xlnT", c % 2, dc), OV], writes=[("ps", bk)], name="qkv")

        def stageB(oc, c, cpar, cs):
            bk = oc % 2
            if oc < 16:
                qk = oc % 2
                sch.add("ACT", lambda e, oc=oc, bk=bk, qk=qk: e.activation(
                    qkf[:, qk, :], psA(bk), AF.Identity, bias=qkvbT_sb[:, oc:oc + 1]),
                    reads=[("ps", bk), ("qkvbT",)], writes=[("qkf", qk)], name="qkevac")
                sch.add("DVE", lambda e, qk=qk, cpar=cpar: e.tensor_tensor(
                    tsin[:], qkf[:, qk, :], sinb[:, cpar, :], ALU.mult),
                    reads=[("qkf", qk), ("sinb", cpar), OV], writes=[("tsin",)], name="tsin")
                sch.add("DVE", lambda e, qk=qk, cpar=cpar: e.tensor_tensor(
                    tcos[:], qkf[:, qk, :], cosb[:, cpar, :], ALU.mult),
                    reads=[("qkf", qk), ("cosb", cpar), OV], writes=[("tcos",)], name="tcos")
                rb = 2 + (oc % 2)
                sch.add("PE", lambda e, rb=rb: e.matmul(psA(rb), rot_sb[:], tsin[:],
                                                        start=True, stop=False),
                        reads=[("rot",), ("tsin",)], writes=[("ps", rb)], name="ropemm1")
                sch.add("PE", lambda e, rb=rb: e.matmul(psA(rb), ident_sb[:], tcos[:],
                                                        start=False, stop=True),
                        reads=[("ident",), ("tcos",)], writes=[("ps", rb)], name="ropemm2")
                if oc < 8:
                    dst, dkey = qres[:, oc, cs], ("qres", oc, c)
                else:
                    ksl = (oc - 8) % 2
                    dst, dkey = kstage[:, ksl, :], ("kstage", ksl)
                sch.add("ACT", lambda e, rb=rb, dst=dst: e.activation(dst, psA(rb), AF.Copy),
                        reads=[("ps", rb), OV], writes=[dkey], name="ropeevac")
                sq = oc % 2
                sch.add("DVE", lambda e, dst=dst, sq=sq: e.tensor_tensor(sqb[:, sq, :], dst, dst, ALU.mult),
                        reads=[dkey], writes=[("sqb", sq)], name="normsq")
                if oc >= 8:
                    ksl = (oc - 8) % 2
                    sch.add("POOL", lambda e, c=c, oc=oc, ksl=ksl: e.dma_start(
                        out=k_dram[:, oc - 8, c * 4:(c + 1) * 4, :],
                        in_=kstage[:, ksl, :].rearrange("p (t f) -> p t f", t=4)),
                        reads=[("kstage", ksl), OV], writes=[("k_dram", c, oc - 8)],
                        dma=True, name=f"kdma{c}_{oc-8}")
            else:
                vo = oc - 16
                sch.add("ACT", lambda e, oc=oc, bk=bk: e.activation(
                    qkf[:, 0, :], psA(bk), AF.Identity, bias=qkvbT_sb[:, oc:oc + 1]),
                    reads=[("ps", bk), ("qkvbT",)], writes=[("qkf", 0)], name="vevac")
                vsl = vo % 2
                for tsub in range(4):
                    vb = tsub % 2
                    sch.add("PE", lambda e, tsub=tsub, vb=vb: e.transpose(
                        o4_bf[:, vb * 1024:vb * 1024 + 128],
                        qkf[:, 0, tsub * 128:(tsub + 1) * 128], ident_sb[:]),
                        reads=[("qkf", 0), ("ident",)], writes=[("ps", 4 + vb)], name="vt")
                    for hh in range(2):
                        sch.add("DVE", lambda e, tsub=tsub, vb=vb, vsl=vsl, hh=hh: e.tensor_copy(
                            vstage[:, vsl, tsub, hh, 0:64],
                            o4_bf[:, vb * 1024 + hh * 64:vb * 1024 + hh * 64 + 64]),
                            reads=[("ps", 4 + vb), ("vones",)],
                            writes=[("vstage", vsl)], name="vc")
                vdw = [("v_dram", c, vo)]
                if c == NBLK - 1 and vo == 7:
                    vdw.append(("kv_fence",))
                sch.add("POOL", lambda e, c=c, vo=vo, vsl=vsl: e.dma_start(
                    out=v_dram[:, c * 4:(c + 1) * 4, 2 * vo:2 * vo + 2, :],
                    in_=vstage[:, vsl, :, :, :]),
                    reads=[("vstage", vsl), OV], writes=vdw,
                    dma=True, name=f"vdma{c}_{vo}")

        def stageC(oc, c):
            sq = oc % 2
            sch.add("PE", lambda e, sq=sq: e.matmul(o4[0:1, 0:512], onesc_sb[:], sqb[:, sq, :],
                                                    start=True, stop=True),
                    reads=[("onesc",), ("sqb", sq)], writes=[("ps", 4)], name="normmm")
            mx, mslot = (qmax, c * 8 + oc) if oc < 8 else (kmax, c * 8 + oc - 8)
            sch.add("DVE", lambda e, mx=mx, mslot=mslot: e.reduce_max(
                mx[:, mslot:mslot + 1], o4[0:1, 0:512], axis=AX.X),
                reads=[("ps", 4)], writes=[("max", mslot, mx is qmax)], name="normmax")

        for oc in range(26):
            if oc < 24:
                stageA(oc)
            if 1 <= oc < 25:
                stageB(oc - 1, c, cpar, cs)
            if oc >= 2 and oc - 2 < 16:
                stageC(oc - 2, c)
    sch.add("DVE", lambda e: e.reduce_max(cval[:, 0:1], qmax[:], axis=AX.X),
            reads=[("max", i, True) for i in range(64)], writes=[("cval",)], name="cq")
    sch.add("DVE", lambda e: e.reduce_max(cval[:, 1:2], kmax[:], axis=AX.X),
            reads=[("max", i, False) for i in range(64)] + [("cval",)], writes=[("cval",)], name="ck")
    sch.add("DVE", lambda e: e.tensor_tensor(veps[0:1, 0:1], cval[:, 0:1], cval[:, 1:2], ALU.mult),
            reads=[("cval",)], writes=[("veps",)], name="cmul")
    newton_rstd(("veps",), 12, slice(0, 1))
    sch.add("DVE", lambda e: e.tensor_tensor(cval[:, 2:3], veps[0:1, 0:1], rst[0:1, 0:1], ALU.mult),
            reads=[("veps",), ("rstd",)], writes=[("cval",)], name="csqrt")
    sch.add("DVE", lambda e: e.tensor_single_scalar(cval[:, 3:4], cval[:, 2:3], -SCALE, ALU.mult),
            reads=[("cval",)], writes=[("cneg",)], name="cns")
    sch.add("PE", lambda e: e.matmul(o4[:, 0:1], onesr128f_sb[:], cval[0:1, 3:4],
                                     start=True, stop=True),
            reads=[("cneg",), ("onesr128f",)], writes=[("ps", 4)], name="cbc")
    sch.add("ACT", lambda e: e.activation(negc[:], o4[:, 0:1], AF.Copy),
            reads=[("ps", 4)], writes=[("negc",)], name="cbcc")

    for dc in range(8):
        ld("POOL", pwT[:, dc, :], pwT_d[dc * 128:(dc + 1) * 128, :], f"pwT{dc}",
           [("pwT", dc), OV])
        ld("POOL", u1[:, dc, :], u1_d[dc * 128:(dc + 1) * 128, :], f"u1{dc}",
           [("u1", dc), OV])
    for rc in range(2):
        ld("POOL", v1[:, rc, :], v1_d[rc * 128:(rc + 1) * 128, :], f"v1{rc}",
           [("v1", rc), OV])
        ld("POOL", v2[:, rc, :], v2_d[rc * 128:(rc + 1) * 128, :], f"v2{rc}",
           [("v2", rc), OV])
    for fc in range(32):
        ld("POOL", u2[:, fc, :], u2_d[fc * 128:(fc + 1) * 128, :], f"u2{fc}",
           [("u2", fc), OV])
    ld("POOL", pb_sb[:], pb_d[:], "pb", [("pb",), OV])
    ld("POOL", b2_sb[:], b2_d[:], "b2", [("b2",), OV, ("wl_fence",)])

    pbring = lambda i: [pbuf0[:, 0:1024], pbuf0[:, 1024:2048],
                        pbuf1[:, 0:1024]][i % 3]
    x2q = []
    for b in range(NBLK if stage >= 2 else 0):
        ibase = b * CB
        njt = 4 * (b + 1)
        njg = b + 1
        for t in range(3):
            gt = b * 4 + t
            ld("SP", pool[:, t % 3, :], x_d[(ibase + t * 128):(ibase + (t + 1) * 128), :],
               f"xr{gt}", [("pool", t % 3)])
        for hp in range(8):
            h0 = 2 * hp

            def kvload(g):
                gs = g % 3
                ld("SP", kj[:, gs, :, :],
                   k_dram[:, hp, g * 4:(g + 1) * 4, :],
                   f"kj{b}_{hp}_{g}", [("kj", gs)],
                   reads=[("k_dram", g, hp), ("kv_fence",), ("wl_fence",)])
                ld("SP", vj[:, gs, :, :, :],
                   v_dram[:, g * 4:(g + 1) * 4, h0:h0 + 2, :],
                   f"vj{b}_{hp}_{g}", [("vj", gs)],
                   reads=[("v_dram", g, hp), ("kv_fence",), ("wl_fence",)])

            kvload(0)
            if "mm1" not in parts:
                continue
            next_g = [1]
            for j in range(njt + 2):
                while next_g[0] < njg and 4 * next_g[0] - 6 <= j:
                    kvload(next_g[0])
                    next_g[0] += 1
                if j < njt:
                    jslot = (j // 4) % 3
                    jj = j % 4
                    sset = j % 3
                    sv = [sAB[:, 0:1024], sAB[:, 1024:2048], o4[:, 1024:2048]][sset]
                    sk = [(0, 1), (2, 3), (6, 7)][sset]
                    for hh in range(2):
                        base = hh * 64
                        sch.add("PE", lambda e, jslot=jslot, jj=jj, hh=hh, base=base,
                                sv=sv, hp=hp, ibase=ibase: e.matmul(
                            sv[:, hh * 512:(hh + 1) * 512],
                            kj[base:base + 64, jslot, jj, :],
                            qres[base:base + 64, hp, ibase:ibase + 512],
                            start=True, stop=True,
                            tile_position=(base, 0)),
                            reads=[("kj", jslot), ("qres", hp, b)],
                            writes=[("ps", sk[hh])], name="mm1")
                je = j - 1
                if 0 <= je < njt and "exp" in parts:
                    sset = je % 3
                    sv = [sAB[:, 0:1024], sAB[:, 1024:2048], o4[:, 1024:2048]][sset]
                    sk = [(0, 1), (2, 3), (6, 7)][sset]
                    pslot = je % 3
                    off = max(0, (je - 4 * b) * 128)
                    pb_ap = pbring(pslot).rearrange("p (h f) -> p h f", h=2)[:, :, off:]
                    sv_ap = sv.rearrange("p (h f) -> p h f", h=2)[:, :, off:]
                    sch.add("ACT", lambda e, sv_ap=sv_ap, pb_ap=pb_ap: e.activation(
                        pb_ap, sv_ap, AF.Exp, bias=negc[:], scale=SCALE),
                        reads=[("ps", sk[0]), ("ps", sk[1]), ("negc",)],
                        writes=[("pb", pslot)], name="exp")
                    if je >= 4 * b and "mask" in parts:
                        o = je - 4 * b
                        ms = je % 2
                        for hh in range(2):
                            sch.add("DVE", lambda e, pslot=pslot, hh=hh, o=o, ms=ms: e.tensor_tensor(
                                pmb[:, ms, hh * 512:(hh + 1) * 512],
                                pbring(pslot)[:, hh * 512:(hh + 1) * 512],
                                masks_sb[:, o, :], ALU.mult),
                                reads=[("pb", pslot), ("masks",)],
                                writes=[("pm", ms, hh)], name="mask")
                jm = j - 2
                if 0 <= jm < njt and "mm2" in parts:
                    jslot = (jm // 4) % 3
                    jj = jm % 4
                    pslot = jm % 3
                    diag = (jm >= 4 * b) and ("mask" in parts)
                    ms = jm % 2
                    srcs, skeys = [], []
                    for hh in range(2):
                        if diag:
                            srcs.append(pmb[:, ms, hh * 512:(hh + 1) * 512])
                            skeys.append([("pm", ms, hh)])
                        else:
                            srcs.append(pbring(pslot)[:, hh * 512:(hh + 1) * 512])
                            skeys.append([("pb", pslot)])
                    sch.add("PE", lambda e, jslot=jslot, jj=jj, jm=jm, njt=njt, src=srcs[0]: e.matmul(
                        o4[0:64, 0:512],
                        vj[:, jslot, jj, 0, 0:64], src,
                        start=(jm == 0), stop=(jm == njt - 1),
                        tile_position=(0, 0)),
                        reads=[("vj", jslot)] + skeys[0],
                        writes=[("ps", 4)], name="mm2e")
                    sch.add("PE", lambda e, jslot=jslot, jj=jj, jm=jm, njt=njt, src=srcs[1]: e.matmul(
                        o4[64:128, 0:512],
                        vj[:, jslot, jj, 1, 0:64], src,
                        start=(jm == 0), stop=(jm == njt - 1),
                        tile_position=(0, 64)),
                        reads=[("vj", jslot)] + skeys[1],
                        writes=[("ps", 4)], name="mm2o")
                    sch.add("PE", lambda e, jm=jm, njt=njt, src=srcs[0]: e.matmul(
                        o4[0:1, 512:1024],
                        onesc_sb[:], src,
                        start=(jm == 0), stop=(jm == njt - 1),
                        tile_position=(0, 0)),
                        reads=[("onesc",)] + skeys[0],
                        writes=[("ps", 5)], name="mm2s")
                    sch.add("PE", lambda e, jm=jm, njt=njt, src=srcs[1]: e.matmul(
                        o4[32:33, 512:1024],
                        onesc_sb[:], src,
                        start=(jm == 0), stop=(jm == njt - 1),
                        tile_position=(0, 32)),
                        reads=[("onesc",)] + skeys[1],
                        writes=[("ps", 5)], name="mm2s")
            if 'oevac' in parts and 'mm2' in parts:
                for hh in range(2):
                    sums_ap = o4[0:1, 512:1024] if hh == 0 else o4[32:33, 512:1024]
                    sch.add("ACT", lambda e, sums_ap=sums_ap, hh=hh: e.activation(
                        sbrec2[:, hh, :], sums_ap, AF.Ln),
                        reads=[("ps", 5)], writes=[("recf", hh)], name="orec_ln")
                for hh in range(2):
                    obase = hh * 64
                    o_ap = o4[obase:obase + 64, 0:512]
                    sch.add("ACT", lambda e, hh=hh: e.activation(
                        recb[:], sbrec2[:, hh, :], AF.Exp, scale=-1.0),
                        reads=[("recf", hh)], writes=[("recb",)], name="orec_exp")
                    bcap = o4[obase:obase + 64, 512:1024]
                    sch.add("PE", lambda e, bcap=bcap, obase=obase: e.matmul(
                        bcap, onesr64_sb[:], recb[:], start=True, stop=True,
                        tile_position=(0, obase)),
                        reads=[("recb",), ("onesr64",)], writes=[("ps", 5)], name="obc")
                    sch.add("DVE", lambda e, bcap=bcap, obase=obase: e.tensor_copy(
                        sbrec[obase:obase + 64, :], bcap),
                        reads=[("ps", 5)], writes=[("sbrec",)], name="obcc")
                    sch.add("DVE", lambda e, o_ap=o_ap, hp_g=hp, obase=obase: e.tensor_tensor(
                        OT[obase:obase + 64, hp_g, :], o_ap, sbrec[obase:obase + 64, :], ALU.mult),
                            reads=[("ps", 4), ("sbrec",)],
                            writes=[("OT", hp, obase)], name="odiv")
        if b == 0 and "x2e" in dbg:
            x2e_d = nc.dram_tensor("dump_x2e", [512, D], F32, kind="ExternalOutput").ap()
            sch.add("POOL", lambda e, x2e_d=x2e_d: e.dma_start(out=x2e_d, in_=x2_dram[0:512, :]),
                    reads=[("x2_dram", gt) for gt in range(4)],
                    writes=[("dump", "x2e")], dma=True, name="x2edma")
        if "block" in barriers:
            barrier(f"blk{b}")
        for t in (range(4) if ('proj' in parts and 'oevac' in parts and 'mm2' in parts) else ()):
            use_pb = "projb" in parts
            for dc in range(8):
                for half in range(2):
                    sch.add("PE", lambda e, t=t, dc=dc, half=half, use_pb=use_pb: e.matmul(
                        o4[:, 1024 + half * 512:1536 + half * 512], OT[:, dc, t * 128:(t + 1) * 128],
                        pwT[:, dc, half * 512:(half + 1) * 512],
                        start=(dc == 0), stop=(dc == 7 and not use_pb)),
                        reads=[("OT", dc, 0), ("OT", dc, 64), ("pwT", dc), ("wl_fence",)],
                        writes=[("ps", 6 + half)], name="proj")
            for half in (range(2) if use_pb else ()):
                sch.add("PE", lambda e, half=half: e.matmul(
                    o4[:, 1024 + half * 512:1536 + half * 512], onesr128b_sb[:], pb_sb[:, half * 512:(half + 1) * 512],
                    start=False, stop=True),
                    reads=[("pb",), ("onesr128b",), ("wl_fence",)], writes=[("ps", 6 + half)], name="projb")
            if "ln2" not in parts:
                continue
            pk = [("ps", 6), ("ps", 7)]
            o2ap = o4[:, 1024:2048]
            if t == 0 and b == 0 and "prj" in dbg:
                prj_d = nc.dram_tensor("dump_prj", [128, D], F32, kind="ExternalOutput").ap()
                sch.add("DVE", lambda e: e.tensor_copy(x2o[:, 1, :], o2ap),
                        reads=pk, writes=[("x2o", 1), ("dbgprj",)], name="prjcap")
                sch.add("POOL", lambda e, prj_d=prj_d: e.dma_start(out=prj_d, in_=x2o[:, 1, :]),
                        reads=[("dbgprj",), ("x2o", 1)], writes=[("dump", "prj")],
                        dma=True, name="prjdma")
            sch.add("DVE", lambda e, o2ap=o2ap: e.reduce_sum(
                nsum[:, 0:1], o2ap, axis=AX.X, negate=True),
                reads=pk, writes=[("nsum",)], name="ln2ns")
            sch.add("ACT", lambda e, o2ap=o2ap: e.activation(
                devnull[:], o2ap, AF.Square, accum_out=sumsq[:, 0:1]),
                reads=pk, writes=[("sumsq",), ("devnull",)], name="ln2sq")
            ln_chunk_rstd(NEWTON["ln2"], nt=1)
            xo = t % 2
            sch.add("DVE", lambda e, o2ap=o2ap: e.tensor_scalar(
                afft[:], o2ap, rst[:, 0:1], lbias[:, 0:1], ALU.mult, ALU.add),
                reads=pk + [("rstd",), ("lbias",)], writes=[("afft",)], name="ln2app_a")
            sch.add("DVE", lambda e, t=t, xo=xo: e.tensor_tensor(
                x2o[:, xo, :], afft[:], pool[:, t % 3, :], ALU.add),
                reads=[("afft",), ("pool", t % 3)],
                writes=[("x2o", xo)], name="ln2app_b")
            gt = b * 4 + t
            x2w = [("x2_dram", gt)]
            if b == NBLK - 1 and t == 3:
                x2w.append(("x2_fence",))
            def x2st(e=None, gt=gt, xo=xo, x2w=x2w):
                sch.add("SP", lambda e, gt=gt, xo=xo: e.dma_start(
                    out=x2_dram[gt * 128:(gt + 1) * 128, :], in_=x2o[:, xo, :]),
                    reads=[("x2o", xo)], writes=x2w, dma=True, name=f"x2st{gt}")
            if x2q:
                x2q.pop(0)()
            x2q.append(x2st)
            if t == 3:
                x2q.pop(0)()
            if t == 0:
                ld("SP", pool[:, 0, :], x_d[(ibase + 3 * 128):(ibase + 4 * 128), :],
                   f"xr{b * 4 + 3}", [("pool", 0)])
        if b == 0 and "x2e" in dbg:
            x2e_d = nc.dram_tensor("dump_x2e", [512, D], F32, kind="ExternalOutput").ap()
            sch.add("POOL", lambda e, x2e_d=x2e_d: e.dma_start(out=x2e_d, in_=x2_dram[0:512, :]),
                    reads=[("x2_dram", gt) for gt in range(4)],
                    writes=[("dump", "x2e")], dma=True, name="x2edma")
        if "block" in barriers:
            barrier(f"blk{b}")

    for c in range(NBLK if stage >= 3 else 0):
        for pair in range(2):
            for t in (2 * pair, 2 * pair + 1):
                gt = c * 4 + t
                ld("SP", pool[:, t % 3, :], x2_dram[gt * 128:(gt + 1) * 128, :],
                   f"x2l{gt}", [("pool", t % 3)], reads=[("x2_dram", gt), ("x2_fence",)])
            for t in (2 * pair, 2 * pair + 1):
                ln_stats_tile(pool[:, t % 3, :], t % 2, ("pool", t % 3))
            ln_chunk_rstd(NEWTON["ln3"], nt=2)
            for t in (2 * pair, 2 * pair + 1):
                tp = t % 2
                sch.add("DVE", lambda e, t=t, tp=tp: e.tensor_scalar(
                    xln[:, tp, :], pool[:, t % 3, :], rst[:, tp:tp + 1], lbias[:, tp:tp + 1], ALU.mult, ALU.add),
                    reads=[("pool", t % 3), ("rstd",), ("lbias",)], writes=[("xln", tp)], name="ln3app")
                for dc in range(8):
                    bpar = dc % 2
                    sch.add("PE", lambda e, tp=tp, dc=dc, bpar=bpar: e.transpose(
                        o4_bf[:, (2 + bpar) * 1024:(2 + bpar) * 1024 + 128],
                        xln[:, tp, dc * 128:(dc + 1) * 128], ident_sb[:]),
                        reads=[("xln", tp), ("ident",)], writes=[("ps", 6 + bpar)], name="x2lt")
                    sch.add("DVE", lambda e, t=t, dc=dc, bpar=bpar, c=c: e.tensor_copy(
                        XL[c % 2][:, dc, t * 128:(t + 1) * 128],
                        o4_bf[:, (2 + bpar) * 1024:(2 + bpar) * 1024 + 128]),
                        reads=[("ps", 6 + bpar)], writes=[("xlnT", c % 2, dc)], name="x2ltc")
        for rc in range(2):
            for dc in range(8):
                sch.add("PE", lambda e, rc=rc, dc=dc, c=c: e.matmul(
                    psA(2 + rc), u1[:, dc, rc * 128:(rc + 1) * 128], XL[c % 2][:, dc, :],
                    start=(dc == 0), stop=(dc == 7)),
                    reads=[("u1", dc), ("xlnT", c % 2, dc), ("wl_fence",)], writes=[("ps", 2 + rc)], name="u1mm")
            sch.add("DVE", lambda e, rc=rc: e.tensor_copy(ptb[:, rc, :], psA(2 + rc)),
                    reads=[("ps", 2 + rc)], writes=[("ptb", rc)], name="ptevac")
        def v1_gelu(fc):
            hb = fc % 2
            for rc in range(2):
                sch.add("PE", lambda e, fc=fc, rc=rc, hb=hb: e.matmul(
                    psO(hb), v1[:, rc, fc * 128:(fc + 1) * 128], ptb[:, rc, :],
                    start=(rc == 0), stop=(rc == 1)),
                    reads=[("v1", rc), ("ptb", rc), ("wl_fence",)], writes=[("ps", 4 + hb)], name="v1mm")
            sch.add("ACT", lambda e, fc=fc, hb=hb: e.activation(
                htb[:, hb, :], psO(hb), AF.Gelu_apprx_tanh, bias=b1T_sb[:, fc:fc + 1]),
                reads=[("ps", 4 + hb), ("b1T",)], writes=[("htb", hb)], name="gelu")

        def u2_mms(fc):
            hb = fc % 2
            for ts4 in range(4):
                q2ap = [sAB[:, 1024:1280], sAB[:, 1536:1792],
                        o4[:, 1024:1280], o4[:, 1536:1792]][ts4]
                q2bk = [2, 3, 6, 7][ts4]
                sch.add("PE", lambda e, fc=fc, hb=hb, ts4=ts4, q2ap=q2ap: e.matmul(
                    q2ap,
                    htb[:, hb, ts4 * 128:(ts4 + 1) * 128], u2[:, fc, :],
                    start=(fc == 0), stop=(fc == 31)),
                    reads=[("htb", hb), ("u2", fc), ("wl_fence",)], writes=[("ps", q2bk)], name="u2mm")

        for fc in range(33):
            if fc < 32:
                v1_gelu(fc)
            if fc >= 1:
                u2_mms(fc - 1)
        for ts4 in range(4):
            q2ap = [sAB[:, 1024:1280], sAB[:, 1536:1792],
                    o4[:, 1024:1280], o4[:, 1536:1792]][ts4]
            q2bk = [2, 3, 6, 7][ts4]
            sch.add("DVE", lambda e, ts4=ts4, q2ap=q2ap: e.tensor_copy(
                q2b[:, ts4, :], q2ap),
                reads=[("ps", q2bk)], writes=[("q2b", ts4)], name="q2evac")
        for ts4 in range(4):
            for rc in range(2):
                sch.add("PE", lambda e, ts4=ts4, rc=rc: e.transpose(
                    sAB_bf[:, (2 + rc) * 1024:(2 + rc) * 1024 + 128],
                    q2b[:, ts4, rc * 128:(rc + 1) * 128], ident_sb[:]),
                    reads=[("q2b", ts4), ("ident",)], writes=[("ps", 2 + rc)], name="q2t")
                sch.add("DVE", lambda e, ts4=ts4, rc=rc: e.tensor_copy(
                    q2t[:, rc, ts4, :], sAB_bf[:, (2 + rc) * 1024:(2 + rc) * 1024 + 128]),
                    reads=[("ps", 2 + rc)], writes=[("q2tk", rc, ts4)], name="q2tc")
        for t in range(4):
            gt = c * 4 + t
            ld("SP", pool[:, t % 3, :], x2_dram[gt * 128:(gt + 1) * 128, :],
               f"x2r{gt}", [("pool", t % 3)], reads=[("x2_dram", gt), ("x2_fence",)])
            for half in range(2):
                for rc in range(2):
                    sch.add("PE", lambda e, t=t, half=half, rc=rc: e.matmul(
                        psA(half), q2t[:, rc, t, :], v2[:, rc, half * 512:(half + 1) * 512],
                        start=(rc == 0), stop=False),
                        reads=[("q2tk", rc, t), ("v2", rc), ("wl_fence",)],
                        writes=[("ps", half)], name="v2mm")
                sch.add("PE", lambda e, half=half: e.matmul(
                    psA(half), onesr128b_sb[:], b2_sb[:, half * 512:(half + 1) * 512],
                    start=False, stop=True),
                    reads=[("b2",), ("onesr128b",), ("wl_fence",)], writes=[("ps", half)], name="v2b")
            pk = [("ps", 0), ("ps", 1)]
            o2ap = sAB[:, 0:1024]
            sch.add("DVE", lambda e, o2ap=o2ap: e.reduce_sum(
                nsum[:, 0:1], o2ap, axis=AX.X, negate=True),
                reads=pk, writes=[("nsum",)], name="ln4ns")
            sch.add("ACT", lambda e, o2ap=o2ap: e.activation(
                devnull[:], o2ap, AF.Square, accum_out=sumsq[:, 0:1]),
                reads=pk, writes=[("sumsq",), ("devnull",)], name="ln4sq")
            ln_chunk_rstd(NEWTON["ln4"], nt=1)
            xo = t % 2
            sch.add("DVE", lambda e, o2ap=o2ap: e.tensor_scalar(
                afft[:], o2ap, rst[:, 0:1], lbias[:, 0:1], ALU.mult, ALU.add),
                reads=pk + [("rstd",), ("lbias",)], writes=[("afft",)], name="ln4app_a")
            sch.add("DVE", lambda e, t=t, xo=xo: e.tensor_tensor(
                x2o[:, xo, :], afft[:], pool[:, t % 3, :], ALU.add),
                reads=[("afft",), ("pool", t % 3)],
                writes=[("x2o", xo)], name="ln4app_b")
            gt = c * 4 + t
            sch.add("POOL", lambda e, gt=gt, xo=xo: e.dma_start(
                out=out_d[gt * 128:(gt + 1) * 128, :], in_=x2o[:, xo, :]),
                reads=[("x2o", xo)], writes=[("outd", gt)], dma=True, name=f"outst{gt}")

    dump_specs = {
        "ptbd": (ptb[:], [128, 2, 512], BF16, [("ptb", rc) for rc in range(2)]),
        "htbd": (htb[:], [128, 2, 512], BF16, [("htb", hb) for hb in range(2)]),
        "q2bd": (q2b[:], [128, 4, 256], BF16, [("q2b", i) for i in range(4)]),
        "qres": (qres[:], [128, HP, S], BF16, [("qres", hp, c) for hp in range(8) for c in range(8)]),
        "negc_d": (negc[:], [128, 1], F32, [("negc",)]),
        "kd": (k_dram[:], [128, HP, NJT, 128], BF16,
               [("k_dram", c, hp) for c in range(8) for hp in range(8)]),
        "vd": (v_dram[:], [128, NJT, H, 65], BF16,
               [("v_dram", c, vo) for c in range(8) for vo in range(8)]),
        "x2d": (x2_dram[:], [S, D], F32, [("x2_dram", gt) for gt in range(32)]),
        "otd": (OT[:], [128, HP, 512], BF16,
                [("OT", hp, bb) for hp in range(8) for bb in (0, 64)]),
    }
    final_reads = [("outd", gt) for gt in range(32)] if stage >= 3 else []
    if stage == 2 and not ("proj" in parts and "ln2" in parts and "oevac" in parts and "mm2" in parts):
        final_reads = []
    for dn in dbg:
        if dn == "x2e":
            final_reads.append(("dump", "x2e"))
            continue
        if dn == "prj":
            final_reads.append(("dump", "prj"))
            continue
        ap, shp, dt, keys = dump_specs[dn]
        dd = nc.dram_tensor("dump_" + dn, shp, dt, kind="ExternalOutput").ap()
        sch.add("POOL", lambda e, dd=dd, ap=ap: e.dma_start(out=dd, in_=ap),
                reads=keys, writes=[("dump", dn)], dma=True, name="dump" + dn)
        final_reads.append(("dump", dn))
    if stage < 3 and not dbg:
        final_reads += [("k_dram", c, hp) for c in range(8) for hp in range(8)]

    sch.add("SP", lambda e: e.nop(), reads=final_reads, writes=[], name="final")

    sch.finalize()
    return sch


def kernel(L0_attn_x, L0_attn_qkv_w, L0_attn_qkv_b, L0_attn_proj_w, L0_attn_proj_b,
           L0_ffn_U1, L0_ffn_V1, L0_ffn_b1, L0_ffn_U2, L0_ffn_V2, L0_ffn_b2):
    x = np.ascontiguousarray(np.asarray(L0_attn_x, dtype=np.float32))
    consts = host_consts()
    common = {
        "wqT": np.ascontiguousarray(np.asarray(L0_attn_qkv_w, np.float32).T),
        "qkvbT": np.ascontiguousarray(np.asarray(L0_attn_qkv_b, np.float32).reshape(24, 128).T),
        "pwT": np.ascontiguousarray(np.asarray(L0_attn_proj_w, np.float32).T),
        "pb": np.asarray(L0_attn_proj_b, np.float32).reshape(1, D),
        "u1": np.ascontiguousarray(np.asarray(L0_ffn_U1, np.float32)),
        "v1": np.ascontiguousarray(np.asarray(L0_ffn_V1, np.float32)),
        "b1T": np.ascontiguousarray(np.asarray(L0_ffn_b1, np.float32).reshape(32, 128).T),
        "u2": np.ascontiguousarray(np.asarray(L0_ffn_U2, np.float32)),
        "v2": np.ascontiguousarray(np.asarray(L0_ffn_V2, np.float32)),
        "b2": np.asarray(L0_ffn_b2, np.float32).reshape(1, D),
        "cos_t": consts["cos_t"], "sin_t": consts["sin_t"],
        "rot": consts["rot"], "ident": consts["ident"], "masks": consts["masks"],
        "ones_c": consts["ones_c"], "ones_r64": consts["ones_r64"],
        "ones_r128f": consts["ones_r128f"], "ones_r128b": consts["ones_r128b"],
    }

    nc = bass.Bass("TRN2", target_bir_lowering=False, debug=False)
    es = ExitStack()
    with es:
        sch = build(nc, es)
        sems = {}
        for e in ENGINES:
            sems[("tick", e)] = es.enter_context(nc.semaphore(f"tick_{e}"))
            sems[("dma", e)] = es.enter_context(nc.semaphore(f"dma_{e}"))
        block = es.enter_context(nc.Block())
        block.sync(lambda e: sch.emit("SP", e, sems))
        block.tensor(lambda e: sch.emit("PE", e, sems))
        block.vector(lambda e: sch.emit("DVE", e, sems))
        block.scalar(lambda e: sch.emit("ACT", e, sems))
        block.gpsimd(lambda e: sch.emit("POOL", e, sems))

    in_maps = [dict(common, x=np.ascontiguousarray(x[i])) for i in range(8)]
    res = run_bass_kernel_spmd(nc, in_maps, core_ids=list(range(8)))
    out = np.stack([res.results[i]["out"] for i in range(8)], axis=0)
    return out.astype(np.float32)


# version 29
# speedup vs baseline: 1.1869x; 1.1869x over previous
import numpy as np
import ml_dtypes
import concourse.bass as bass
import concourse.mybir as mybir
from concourse.bass_utils import run_bass_kernel_spmd
from contextlib import ExitStack
from collections import deque

F32 = mybir.dt.float32
BF16 = mybir.dt.bfloat16
AF = mybir.ActivationFunctionType
ALU = mybir.AluOpType
AX = mybir.AxisListType

S, D, H, HD, HP = 4096, 1024, 16, 64, 8
DF, RK = 4096, 256
NBLK, CB, NJT = 8, 512, 32
EPS = 1e-5
SCALE = HD ** -0.5
NEWTON = {"ln1": 9, "ln2": 11, "ln3": 9, "ln4": 11}

ENGINES = ("PE", "ACT", "DVE", "POOL", "SP")


class Sched:

    def __init__(self):
        self.ops = []
        self.last_write = {}
        self.readers = {}

    def add(self, engine, fn, reads=(), writes=(), dma=False, name=""):
        idx = len(self.ops)
        waits = set()
        for k in reads:
            w = self.last_write.get(k)
            if w is not None:
                wo = self.ops[w]
                if wo["dma"] or wo["engine"] != engine or engine != "PE":
                    waits.add(w)
            if isinstance(k, tuple) and k and k[0] == "ps":
                for r in self.readers.get(k, ()):
                    if self.ops[r]["engine"] != engine:
                        waits.add(r)
        for k in writes:
            w = self.last_write.get(k)
            if w is not None:
                wo = self.ops[w]
                if wo["dma"] or dma or wo["engine"] != engine or engine != "PE":
                    waits.add(w)
            for r in self.readers.get(k, ()):
                ro = self.ops[r]
                if ro["dma"] or dma or ro["engine"] != engine or engine != "PE":
                    waits.add(r)
        for k in reads:
            self.readers.setdefault(k, []).append(idx)
        for k in writes:
            self.last_write[k] = idx
            self.readers[k] = []
        self.ops.append(dict(engine=engine, fn=fn, waits=sorted(waits),
                             dma=dma, name=name, needs_event=False, event=None))
        return idx

    def finalize(self):
        for op in self.ops:
            for w in op["waits"]:
                self.ops[w]["needs_event"] = True
        tick = {e: 0 for e in ENGINES}
        dmac = {e: 0 for e in ENGINES}
        for op in self.ops:
            if op["dma"]:
                dmac[op["engine"]] += 1
                op["event"] = ("dma", op["engine"], dmac[op["engine"]] * 16)
            elif op["needs_event"]:
                tick[op["engine"]] += 1
                op["event"] = ("tick", op["engine"], tick[op["engine"]])
        self.dma_totals = dmac

    def emit(self, engine, eng, sems):
        hw = {}
        for op in self.ops:
            if op["engine"] != engine:
                continue
            for w in op["waits"]:
                ev = self.ops[w]["event"]
                kind, e2, val = ev
                k2 = (kind, e2)
                if hw.get(k2, 0) >= val:
                    continue
                eng.wait_ge(sems[k2], val)
                hw[k2] = val
            inst = op["fn"](eng)
            ev = op["event"]
            if ev is not None:
                kind, e2, val = ev
                if kind == "dma":
                    inst.then_inc(sems[("dma", e2)], 16)
                else:
                    inst.then_inc(sems[("tick", e2)], 1)


def host_consts():
    pos = np.arange(S, dtype=np.float64)
    invf = 10000.0 ** (-(np.arange(0, HD, 2, dtype=np.float64) / HD))
    a = (np.arange(128) % 64) // 2
    ang = pos[None, :] * invf[a][:, None]
    cos_t = np.cos(ang).astype(ml_dtypes.bfloat16)
    sin_t = np.sin(ang).astype(ml_dtypes.bfloat16)
    rot = np.zeros((128, 128), np.float32)
    for i in range(0, 128, 2):
        rot[i + 1, i] = -1.0
        rot[i, i + 1] = 1.0
    rot = rot.astype(ml_dtypes.bfloat16)
    ident = np.eye(128, dtype=ml_dtypes.bfloat16)
    masks = np.zeros((128, 4, 512), np.float32)
    for o in range(4):
        for j in range(128):
            masks[j, o, o * 128 + j:] = 1.0
    masks = masks.astype(ml_dtypes.bfloat16)
    ones_c = np.ones((128, 1), dtype=ml_dtypes.bfloat16)
    ones_r64 = np.ones((1, 64), dtype=ml_dtypes.bfloat16)
    ones_r128f = np.ones((1, 128), np.float32)
    ones_r128b = np.ones((1, 128), dtype=ml_dtypes.bfloat16)
    return dict(cos_t=cos_t, sin_t=sin_t, rot=rot, ident=ident, masks=masks,
                ones_c=ones_c, ones_r64=ones_r64, ones_r128f=ones_r128f,
                ones_r128b=ones_r128b)


def build(nc, es, stage=3, dbg=(), nattn=NBLK, parts=frozenset({'mm1','exp','mask','mm2','oevac','proj','projb','ln2'}), barriers=()):
    def din(name, shape, dt=F32):
        return nc.dram_tensor(name, list(shape), dt, kind="ExternalInput").ap()

    x_d = din("x", (S, D))
    wqT_d = din("wqT", (D, 3 * D))
    qkvbT_d = din("qkvbT", (128, 24))
    pwT_d = din("pwT", (D, D))
    pb_d = din("pb", (1, D))
    u1_d = din("u1", (D, RK))
    v1_d = din("v1", (RK, DF))
    b1T_d = din("b1T", (128, 32))
    u2_d = din("u2", (DF, RK))
    v2_d = din("v2", (RK, D))
    b2_d = din("b2", (1, D))
    cos_d = din("cos_t", (128, S), BF16)
    sin_d = din("sin_t", (128, S), BF16)
    rot_d = din("rot", (128, 128), BF16)
    ident_d = din("ident", (128, 128), BF16)
    masks_d = din("masks", (128, 4, 512), BF16)
    onesc_d = din("ones_c", (128, 1), BF16)
    onesr64_d = din("ones_r64", (1, 64), BF16)
    onesr128f_d = din("ones_r128f", (1, 128), F32)
    onesr128b_d = din("ones_r128b", (1, 128), BF16)

    out_d = nc.dram_tensor("out", [S, D], F32, kind="ExternalOutput").ap()

    k_dram = nc.dram_tensor("k_scr", [128, HP, NJT, 128], BF16, kind="Internal").ap()
    v_dram = nc.dram_tensor("v_scr", [128, NJT, H, 65], BF16, kind="Internal").ap()
    x2_dram = nc.dram_tensor("x2_scr", [S, D], F32, kind="Internal").ap()

    cur = [16512]
    total_budget = 229376

    def sbuf(name, shape, dt, at=None):
        nbytes = int(np.prod(shape[1:])) * (2 if dt == BF16 else 4)
        nbytes = (nbytes + 31) // 32 * 32
        if at is None:
            off = cur[0]
            cur[0] += nbytes
        else:
            off = at
        assert off + nbytes <= total_budget, f"sbuf overflow at {name}: {off+nbytes}"
        return nc.alloc_sbuf_tensor_at(name, list(shape), dt, offset=off).ap()

    qres = sbuf("qres", (128, HP, S), BF16)
    OT = sbuf("OT", (128, HP, 512), BF16)
    pb0_off = cur[0]
    pbuf0 = sbuf("pbuf0", (128, 2048), BF16)
    pbuf1 = sbuf("pbuf1", (128, 1024), BF16)
    xlnTB = sbuf("xlnTB", (128, 8, 512), BF16, at=pb0_off)
    pmb = sbuf("pmb", (128, 2, 1024), BF16)
    kj = sbuf("kj", (128, 3, 4, 128), BF16)
    vj = sbuf("vj", (128, 3, 4, 2, 65), BF16)
    pool = sbuf("pool", (128, 3, D), F32)
    x2o = sbuf("x2o", (128, 2, D), F32)
    sbrec = sbuf("sbrec", (128, 512), BF16)
    cosb = sbuf("cosb", (128, 1, 512), BF16)
    sinb = sbuf("sinb", (128, 1, 512), BF16)
    masks_sb = sbuf("masks_sb", (128, 4, 512), BF16)
    rot_sb = sbuf("rot_sb", (128, 128), BF16)
    ident_sb = sbuf("ident_sb", (128, 128), BF16)
    onesc_sb = sbuf("onesc_sb", (128, 1), BF16)
    onesr64_sb = sbuf("onesr64_sb", (1, 64), BF16)
    onesr128f_sb = sbuf("onesr128f_sb", (1, 128), F32)
    onesr128b_sb = sbuf("onesr128b_sb", (1, 128), BF16)
    qkvbT_sb = sbuf("qkvbT_sb", (128, 24), F32)
    b1T_sb = sbuf("b1T_sb", (128, 32), F32)
    pb_sb = sbuf("pb_sb", (1, D), BF16)
    b2_sb = sbuf("b2_sb", (1, D), BF16)
    negc = sbuf("negc", (128, 1), F32)
    recb = sbuf("recb", (1, 512), BF16)
    nsum = sbuf("nsum", (128, 4), F32)
    sumsq = sbuf("sumsq", (128, 4), F32)
    msq = sbuf("msq", (128, 4), F32)
    veps = sbuf("veps", (128, 4), F32)
    rst = sbuf("rst", (128, 4), F32)
    lbias = sbuf("lbias", (128, 4), F32)
    rs1 = sbuf("rs1", (128, 4), F32)
    rs2 = sbuf("rs2", (128, 4), F32)
    qmax = sbuf("qmax", (1, 64), F32)
    kmax = sbuf("kmax", (1, 64), F32)
    cval = sbuf("cval", (1, 4), F32)
    devnull = sbuf("devnull", (128, 1024), BF16)
    xln = sbuf("xln", (128, 2, D), BF16)
    xlnT = sbuf("xlnT", (128, 8, 512), BF16)
    afft = sbuf("afft", (128, D), F32)

    ov_base = cur[0]
    wqT = sbuf("wqT_sb", (128, 8, 3 * D), BF16, at=ov_base)
    qkf = sbuf("qkf", (128, 2, 512), BF16, at=ov_base + 49152)
    tsin = sbuf("tsin", (128, 512), BF16, at=ov_base + 51200)
    tcos = sbuf("tcos", (128, 512), BF16, at=ov_base + 52224)
    sqb = sbuf("sqb", (128, 2, 512), BF16, at=ov_base + 53248)
    kstage = sbuf("kstage", (128, 2, 512), BF16, at=ov_base + 55296)
    vstage = sbuf("vstage", (128, 2, 4, 2, 65), BF16, at=ov_base + 57344)
    p1_end = ov_base + 57344 + 2 * 4 * 2 * 65 * 2
    pwT = sbuf("pwT_sb", (128, 8, D), BF16, at=ov_base)
    u1 = sbuf("u1_sb", (128, 8, RK), BF16, at=ov_base + 16384)
    v1 = sbuf("v1_sb", (128, 2, DF), BF16, at=ov_base + 20480)
    u2 = sbuf("u2_sb", (128, 32, RK), BF16, at=ov_base + 36864)
    v2 = sbuf("v2_sb", (128, 2, D), BF16, at=ov_base + 53248)
    ptb = sbuf("ptb", (128, 2, 512), BF16, at=ov_base + 57344)
    htb = sbuf("htb", (128, 2, 512), BF16, at=ov_base + 59392)
    q2b = sbuf("q2b", (128, 4, 256), BF16, at=ov_base + 61440)
    q2t = sbuf("q2t", (128, 2, 4, 128), BF16, at=ov_base + 63488)
    a_end = ov_base + 63488 + 2048
    sbrec2 = sbuf("sbrec2", (1, 2, 512), F32, at=ov_base + 59424)
    cur[0] = max(p1_end, a_end)
    assert cur[0] <= total_budget, f"total sbuf {cur[0]}"

    sAB = es.enter_context(nc.psum_tensor("sAB", [128, 2048], F32)).ap()
    o4 = es.enter_context(nc.psum_tensor("o4", [128, 2048], F32)).ap()

    def psA(bank):
        return sAB[:, bank * 512:(bank + 1) * 512]

    def psO(bank):
        return o4[:, bank * 512:(bank + 1) * 512]

    sAB_bf = sAB.bitcast(BF16)
    o4_bf = o4.bitcast(BF16)

    XL = [xlnT, xlnTB]

    sch = Sched()
    OV = "OVP1"
    barn = [0]

    def barrier(tag):
        barn[0] += 1
        bid = barn[0]
        for e in ENGINES:
            sch.add(e, lambda x: x.nop(), reads=(), writes=[("bar1", bid, e)],
                    name=f"bar1_{tag}_{e}")
        for e in ENGINES:
            sch.add(e, lambda x: x.nop(),
                    reads=[("bar1", bid, e2) for e2 in ENGINES],
                    writes=(), name=f"bar2_{tag}_{e}")

    def lp(f):
        def g(e):
            with nc.allow_low_precision(reason="softmax 1/sum broadcast in bf16; "
                                        "same rounding class as bf16 matmul operands"):
                return f(e)
        return g

    def ld(engine, dst, src, name, writes, reads=()):
        sch.add(engine, lambda e, d=dst, s=src: e.dma_start(out=d, in_=s),
                reads=reads, writes=writes, dma=True, name=name)

    ld("SP", masks_sb[:], masks_d[:], "masks", [("masks",)])
    ld("SP", rot_sb[:], rot_d[:], "rot", [("rot",)])
    ld("SP", ident_sb[:], ident_d[:], "ident", [("ident",)])
    ld("SP", onesc_sb[:], onesc_d[:], "onesc", [("onesc",)])
    ld("SP", onesr64_sb[:], onesr64_d[:], "onesr64", [("onesr64",)])
    ld("SP", onesr128f_sb[:], onesr128f_d[:], "onesr128f", [("onesr128f",)])
    ld("SP", onesr128b_sb[:], onesr128b_d[:], "onesr128b", [("onesr128b",)])
    ld("SP", qkvbT_sb[:], qkvbT_d[:], "qkvbT", [("qkvbT",)])
    ld("SP", b1T_sb[:], b1T_d[:], "b1T", [("b1T",)])
    for dc in range(8):
        wk = [("wqT", dc), OV] + ([("wq_fence",)] if dc == 7 else [])
        ld("POOL", wqT[:, dc, :], wqT_d[dc * 128:(dc + 1) * 128, :],
           f"wqT{dc}", wk)
    sch.add("DVE", lambda e: e.memset(pbuf0[:], 0.0), reads=(), writes=[("pbz0",)], name="pbz0")
    sch.add("DVE", lambda e: e.memset(pbuf1[:], 0.0), reads=[("pbz0",)], writes=[("pb", 0), ("pb", 1), ("pb", 2)], name="pbz1")
    sch.add("DVE", lambda e: e.memset(vstage[:, :, :, :, 64:65], 1.0),
            reads=[OV], writes=[("vones",)], name="vones")

    def newton_rstd(src_key, n_iter, wslice):
        V = veps[:, wslice]
        sch.add("ACT", lambda e: e.activation(rs1[:, wslice], V, AF.Ln),
                reads=[src_key], writes=[("rs1",)], name="rstd_ln")
        sch.add("ACT", lambda e: e.activation(rst[:, wslice], rs1[:, wslice], AF.Exp, scale=-0.5),
                reads=[("rs1",)], writes=[("rstd",)], name="rstd_exp")

    def ln_stats_tile(xap, tslot, xkey):
        sch.add("DVE", lambda e: e.reduce_sum(nsum[:, tslot:tslot + 1], xap, axis=AX.X, negate=True),
                reads=[xkey], writes=[("nsum",)], name="ln_ns")
        sch.add("ACT", lambda e: e.activation(devnull[:], xap, AF.Square,
                                              accum_out=sumsq[:, tslot:tslot + 1]),
                reads=[xkey], writes=[("sumsq",), ("devnull",)], name="ln_sq")

    def ln_chunk_rstd(n_iter, nt=4):
        sl = slice(0, nt)
        sch.add("DVE", lambda e: e.tensor_single_scalar(msq[:, sl], nsum[:, sl], 1.0 / D, ALU.mult),
                reads=[("nsum",)], writes=[("msq",)])
        sch.add("DVE", lambda e: e.tensor_tensor(rs1[:, sl], msq[:, sl], msq[:, sl], ALU.mult),
                reads=[("msq",)], writes=[("rs1",)])
        sch.add("DVE", lambda e: e.scalar_tensor_tensor(veps[:, sl], sumsq[:, sl], 1.0 / D,
                                                        rs1[:, sl], ALU.mult, ALU.subtract),
                reads=[("sumsq",), ("rs1",)], writes=[("veps",)])
        sch.add("DVE", lambda e: e.tensor_single_scalar(veps[:, sl], veps[:, sl], EPS, ALU.add),
                reads=[("veps",)], writes=[("veps",)])
        newton_rstd(("veps",), n_iter, sl)
        sch.add("DVE", lambda e: e.tensor_tensor(lbias[:, sl], msq[:, sl], rst[:, sl], ALU.mult),
                reads=[("msq",), ("rstd",)], writes=[("lbias",)])

    for c in range(NBLK):
        cs = slice(c * CB, (c + 1) * CB)
        cpar = 0
        ld("SP", cosb[:, cpar, :], cos_d[:, cs], f"cos{c}", [("cosb", cpar), OV])
        ld("SP", sinb[:, cpar, :], sin_d[:, cs], f"sin{c}", [("sinb", cpar), OV])
        for pair in range(2):
            for t in (2 * pair, 2 * pair + 1):
                gt = c * 4 + t
                ld("SP", pool[:, t % 3, :], x_d[gt * 128:(gt + 1) * 128, :], f"x{gt}", [("pool", t % 3)])
            for t in (2 * pair, 2 * pair + 1):
                ln_stats_tile(pool[:, t % 3, :], t % 2, ("pool", t % 3))
            ln_chunk_rstd(NEWTON["ln1"], nt=2)
            for t in (2 * pair, 2 * pair + 1):
                tp = t % 2
                sch.add("DVE", lambda e, t=t, tp=tp: e.tensor_scalar(
                    xln[:, tp, :], pool[:, t % 3, :], rst[:, tp:tp + 1], lbias[:, tp:tp + 1], ALU.mult, ALU.add),
                    reads=[("pool", t % 3), ("rstd",), ("lbias",)], writes=[("xln", tp)], name="ln1app")
                for dc in range(8):
                    bpar = dc % 2
                    sch.add("PE", lambda e, tp=tp, dc=dc, bpar=bpar: e.transpose(
                        o4_bf[:, (2 + bpar) * 1024:(2 + bpar) * 1024 + 128],
                        xln[:, tp, dc * 128:(dc + 1) * 128], ident_sb[:]),
                        reads=[("xln", tp), ("ident",)], writes=[("ps", 6 + bpar)], name="xlt")
                    sch.add("DVE", lambda e, t=t, dc=dc, bpar=bpar, c=c: e.tensor_copy(
                        XL[c % 2][:, dc, t * 128:(t + 1) * 128],
                        o4_bf[:, (2 + bpar) * 1024:(2 + bpar) * 1024 + 128]),
                        reads=[("ps", 6 + bpar)], writes=[("xlnT", c % 2, dc)], name="xltc")
        def stageA(oc):
            bk = oc % 2
            for dc in range(8):
                sch.add("PE", lambda e, oc=oc, dc=dc, bk=bk, c=c: e.matmul(
                    psA(bk), wqT[:, dc, oc * 128:(oc + 1) * 128], XL[c % 2][:, dc, :],
                    start=(dc == 0), stop=(dc == 7)),
                    reads=[("wqT", dc), ("wq_fence",), ("xlnT", c % 2, dc), OV], writes=[("ps", bk)], name="qkv")

        def stageB(oc, c, cpar, cs):
            bk = oc % 2
            if oc < 16:
                qk = oc % 2
                sch.add("ACT", lambda e, oc=oc, bk=bk, qk=qk: e.activation(
                    qkf[:, qk, :], psA(bk), AF.Identity, bias=qkvbT_sb[:, oc:oc + 1]),
                    reads=[("ps", bk), ("qkvbT",)], writes=[("qkf", qk)], name="qkevac")
                sch.add("DVE", lambda e, qk=qk, cpar=cpar: e.tensor_tensor(
                    tsin[:], qkf[:, qk, :], sinb[:, cpar, :], ALU.mult),
                    reads=[("qkf", qk), ("sinb", cpar), OV], writes=[("tsin",)], name="tsin")
                sch.add("DVE", lambda e, qk=qk, cpar=cpar: e.tensor_tensor(
                    tcos[:], qkf[:, qk, :], cosb[:, cpar, :], ALU.mult),
                    reads=[("qkf", qk), ("cosb", cpar), OV], writes=[("tcos",)], name="tcos")
                rb = 2 + (oc % 2)
                sch.add("PE", lambda e, rb=rb: e.matmul(psA(rb), rot_sb[:], tsin[:],
                                                        start=True, stop=False),
                        reads=[("rot",), ("tsin",)], writes=[("ps", rb)], name="ropemm1")
                sch.add("PE", lambda e, rb=rb: e.matmul(psA(rb), ident_sb[:], tcos[:],
                                                        start=False, stop=True),
                        reads=[("ident",), ("tcos",)], writes=[("ps", rb)], name="ropemm2")
                if oc < 8:
                    dst, dkey = qres[:, oc, cs], ("qres", oc, c)
                else:
                    ksl = (oc - 8) % 2
                    dst, dkey = kstage[:, ksl, :], ("kstage", ksl)
                sch.add("ACT", lambda e, rb=rb, dst=dst: e.activation(dst, psA(rb), AF.Copy),
                        reads=[("ps", rb), OV], writes=[dkey], name="ropeevac")
                sq = oc % 2
                sch.add("DVE", lambda e, dst=dst, sq=sq: e.tensor_tensor(sqb[:, sq, :], dst, dst, ALU.mult),
                        reads=[dkey], writes=[("sqb", sq)], name="normsq")
                if oc >= 8:
                    ksl = (oc - 8) % 2
                    sch.add("POOL", lambda e, c=c, oc=oc, ksl=ksl: e.dma_start(
                        out=k_dram[:, oc - 8, c * 4:(c + 1) * 4, :],
                        in_=kstage[:, ksl, :].rearrange("p (t f) -> p t f", t=4)),
                        reads=[("kstage", ksl), OV], writes=[("k_dram", c, oc - 8)],
                        dma=True, name=f"kdma{c}_{oc-8}")
            else:
                vo = oc - 16
                sch.add("ACT", lambda e, oc=oc, bk=bk: e.activation(
                    qkf[:, 0, :], psA(bk), AF.Identity, bias=qkvbT_sb[:, oc:oc + 1]),
                    reads=[("ps", bk), ("qkvbT",)], writes=[("qkf", 0)], name="vevac")
                vsl = vo % 2
                for tsub in range(4):
                    vb = tsub % 2
                    sch.add("PE", lambda e, tsub=tsub, vb=vb: e.transpose(
                        o4_bf[:, vb * 1024:vb * 1024 + 128],
                        qkf[:, 0, tsub * 128:(tsub + 1) * 128], ident_sb[:]),
                        reads=[("qkf", 0), ("ident",)], writes=[("ps", 4 + vb)], name="vt")
                    for hh in range(2):
                        sch.add("DVE", lambda e, tsub=tsub, vb=vb, vsl=vsl, hh=hh: e.tensor_copy(
                            vstage[:, vsl, tsub, hh, 0:64],
                            o4_bf[:, vb * 1024 + hh * 64:vb * 1024 + hh * 64 + 64]),
                            reads=[("ps", 4 + vb), ("vones",)],
                            writes=[("vstage", vsl)], name="vc")
                vdw = [("v_dram", c, vo)]
                if c == NBLK - 1 and vo == 7:
                    vdw.append(("kv_fence",))
                sch.add("POOL", lambda e, c=c, vo=vo, vsl=vsl: e.dma_start(
                    out=v_dram[:, c * 4:(c + 1) * 4, 2 * vo:2 * vo + 2, :],
                    in_=vstage[:, vsl, :, :, :]),
                    reads=[("vstage", vsl), OV], writes=vdw,
                    dma=True, name=f"vdma{c}_{vo}")

        def stageC(oc, c):
            sq = oc % 2
            sch.add("PE", lambda e, sq=sq: e.matmul(o4[0:1, 0:512], onesc_sb[:], sqb[:, sq, :],
                                                    start=True, stop=True),
                    reads=[("onesc",), ("sqb", sq)], writes=[("ps", 4)], name="normmm")
            mx, mslot = (qmax, c * 8 + oc) if oc < 8 else (kmax, c * 8 + oc - 8)
            sch.add("DVE", lambda e, mx=mx, mslot=mslot: e.reduce_max(
                mx[:, mslot:mslot + 1], o4[0:1, 0:512], axis=AX.X),
                reads=[("ps", 4)], writes=[("max", mslot, mx is qmax)], name="normmax")

        for oc in range(26):
            if oc < 24:
                stageA(oc)
            if 1 <= oc < 25:
                stageB(oc - 1, c, cpar, cs)
            if oc >= 2 and oc - 2 < 16:
                stageC(oc - 2, c)
    sch.add("DVE", lambda e: e.reduce_max(cval[:, 0:1], qmax[:], axis=AX.X),
            reads=[("max", i, True) for i in range(64)], writes=[("cval",)], name="cq")
    sch.add("DVE", lambda e: e.reduce_max(cval[:, 1:2], kmax[:], axis=AX.X),
            reads=[("max", i, False) for i in range(64)] + [("cval",)], writes=[("cval",)], name="ck")
    sch.add("DVE", lambda e: e.tensor_tensor(veps[0:1, 0:1], cval[:, 0:1], cval[:, 1:2], ALU.mult),
            reads=[("cval",)], writes=[("veps",)], name="cmul")
    newton_rstd(("veps",), 12, slice(0, 1))
    sch.add("DVE", lambda e: e.tensor_tensor(cval[:, 2:3], veps[0:1, 0:1], rst[0:1, 0:1], ALU.mult),
            reads=[("veps",), ("rstd",)], writes=[("cval",)], name="csqrt")
    sch.add("DVE", lambda e: e.tensor_single_scalar(cval[:, 3:4], cval[:, 2:3], -SCALE, ALU.mult),
            reads=[("cval",)], writes=[("cneg",)], name="cns")
    sch.add("PE", lambda e: e.matmul(o4[:, 0:1], onesr128f_sb[:], cval[0:1, 3:4],
                                     start=True, stop=True),
            reads=[("cneg",), ("onesr128f",)], writes=[("ps", 4)], name="cbc")
    sch.add("ACT", lambda e: e.activation(negc[:], o4[:, 0:1], AF.Copy),
            reads=[("ps", 4)], writes=[("negc",)], name="cbcc")

    for dc in range(8):
        ld("POOL", pwT[:, dc, :], pwT_d[dc * 128:(dc + 1) * 128, :], f"pwT{dc}",
           [("pwT", dc), OV])
        ld("POOL", u1[:, dc, :], u1_d[dc * 128:(dc + 1) * 128, :], f"u1{dc}",
           [("u1", dc), OV])
    for rc in range(2):
        ld("POOL", v1[:, rc, :], v1_d[rc * 128:(rc + 1) * 128, :], f"v1{rc}",
           [("v1", rc), OV])
        ld("POOL", v2[:, rc, :], v2_d[rc * 128:(rc + 1) * 128, :], f"v2{rc}",
           [("v2", rc), OV])
    for fc in range(32):
        ld("POOL", u2[:, fc, :], u2_d[fc * 128:(fc + 1) * 128, :], f"u2{fc}",
           [("u2", fc), OV])
    ld("POOL", pb_sb[:], pb_d[:], "pb", [("pb",), OV])
    ld("POOL", b2_sb[:], b2_d[:], "b2", [("b2",), OV, ("wl_fence",)])

    pbring = lambda i: [pbuf0[:, 0:1024], pbuf0[:, 1024:2048],
                        pbuf1[:, 0:1024]][i % 3]
    x2q = []
    oq = deque()

    def opops(n):
        for _ in range(n):
            if not oq:
                return
            oq.popleft()()
    for b in range(NBLK if stage >= 2 else 0):
        ibase = b * CB
        njt = 4 * (b + 1)
        njg = b + 1
        for t in range(3):
            gt = b * 4 + t
            ld("SP", pool[:, t % 3, :], x_d[(ibase + t * 128):(ibase + (t + 1) * 128), :],
               f"xr{gt}", [("pool", t % 3)])
        for hp in range(8):
            h0 = 2 * hp

            def kvload(g):
                gs = g % 3
                ld("SP", kj[:, gs, :, :],
                   k_dram[:, hp, g * 4:(g + 1) * 4, :],
                   f"kj{b}_{hp}_{g}", [("kj", gs)],
                   reads=[("k_dram", g, hp), ("kv_fence",), ("wl_fence",)])
                ld("SP", vj[:, gs, :, :, :],
                   v_dram[:, g * 4:(g + 1) * 4, h0:h0 + 2, :],
                   f"vj{b}_{hp}_{g}", [("vj", gs)],
                   reads=[("v_dram", g, hp), ("kv_fence",), ("wl_fence",)])

            kvload(0)
            if "mm1" not in parts:
                continue
            next_g = [1]
            for j in range(njt + 2):
                opops(2)
                while next_g[0] < njg and 4 * next_g[0] - 6 <= j:
                    kvload(next_g[0])
                    next_g[0] += 1
                if j < njt:
                    jslot = (j // 4) % 3
                    jj = j % 4
                    sset = j % 3
                    sv = [sAB[:, 0:1024], sAB[:, 1024:2048], o4[:, 1024:2048]][sset]
                    sk = [(0, 1), (2, 3), (6, 7)][sset]
                    for hh in range(2):
                        base = hh * 64
                        sch.add("PE", lambda e, jslot=jslot, jj=jj, hh=hh, base=base,
                                sv=sv, hp=hp, ibase=ibase: e.matmul(
                            sv[:, hh * 512:(hh + 1) * 512],
                            kj[base:base + 64, jslot, jj, :],
                            qres[base:base + 64, hp, ibase:ibase + 512],
                            start=True, stop=True,
                            tile_position=(base, 0)),
                            reads=[("kj", jslot), ("qres", hp, b)],
                            writes=[("ps", sk[hh])], name="mm1")
                je = j - 1
                if 0 <= je < njt and "exp" in parts:
                    sset = je % 3
                    sv = [sAB[:, 0:1024], sAB[:, 1024:2048], o4[:, 1024:2048]][sset]
                    sk = [(0, 1), (2, 3), (6, 7)][sset]
                    pslot = je % 3
                    off = max(0, (je - 4 * b) * 128)
                    pb_ap = pbring(pslot).rearrange("p (h f) -> p h f", h=2)[:, :, off:]
                    sv_ap = sv.rearrange("p (h f) -> p h f", h=2)[:, :, off:]
                    sch.add("ACT", lambda e, sv_ap=sv_ap, pb_ap=pb_ap: e.activation(
                        pb_ap, sv_ap, AF.Exp, bias=negc[:], scale=SCALE),
                        reads=[("ps", sk[0]), ("ps", sk[1]), ("negc",)],
                        writes=[("pb", pslot)], name="exp")
                    if je >= 4 * b and "mask" in parts:
                        o = je - 4 * b
                        ms = je % 2
                        for hh in range(2):
                            sch.add("DVE", lambda e, pslot=pslot, hh=hh, o=o, ms=ms: e.tensor_tensor(
                                pmb[:, ms, hh * 512:(hh + 1) * 512],
                                pbring(pslot)[:, hh * 512:(hh + 1) * 512],
                                masks_sb[:, o, :], ALU.mult),
                                reads=[("pb", pslot), ("masks",)],
                                writes=[("pm", ms, hh)], name="mask")
                jm = j - 2
                if 0 <= jm < njt and "mm2" in parts:
                    jslot = (jm // 4) % 3
                    jj = jm % 4
                    pslot = jm % 3
                    diag = (jm >= 4 * b) and ("mask" in parts)
                    ms = jm % 2
                    srcs, skeys = [], []
                    for hh in range(2):
                        if diag:
                            srcs.append(pmb[:, ms, hh * 512:(hh + 1) * 512])
                            skeys.append([("pm", ms, hh)])
                        else:
                            srcs.append(pbring(pslot)[:, hh * 512:(hh + 1) * 512])
                            skeys.append([("pb", pslot)])
                    sch.add("PE", lambda e, jslot=jslot, jj=jj, jm=jm, njt=njt, src=srcs[0]: e.matmul(
                        o4[0:64, 0:512],
                        vj[:, jslot, jj, 0, 0:64], src,
                        start=(jm == 0), stop=(jm == njt - 1),
                        tile_position=(0, 0)),
                        reads=[("vj", jslot)] + skeys[0],
                        writes=[("ps", 4)], name="mm2e")
                    sch.add("PE", lambda e, jslot=jslot, jj=jj, jm=jm, njt=njt, src=srcs[1]: e.matmul(
                        o4[64:128, 0:512],
                        vj[:, jslot, jj, 1, 0:64], src,
                        start=(jm == 0), stop=(jm == njt - 1),
                        tile_position=(0, 64)),
                        reads=[("vj", jslot)] + skeys[1],
                        writes=[("ps", 4)], name="mm2o")
                    sch.add("PE", lambda e, jm=jm, njt=njt, src=srcs[0]: e.matmul(
                        o4[0:1, 512:1024],
                        onesc_sb[:], src,
                        start=(jm == 0), stop=(jm == njt - 1),
                        tile_position=(0, 0)),
                        reads=[("onesc",)] + skeys[0],
                        writes=[("ps", 5)], name="mm2s")
                    sch.add("PE", lambda e, jm=jm, njt=njt, src=srcs[1]: e.matmul(
                        o4[32:33, 512:1024],
                        onesc_sb[:], src,
                        start=(jm == 0), stop=(jm == njt - 1),
                        tile_position=(0, 32)),
                        reads=[("onesc",)] + skeys[1],
                        writes=[("ps", 5)], name="mm2s")
            def oe_ln(hp=hp):
                for hh in range(2):
                    sums_ap = o4[0:1, 512:1024] if hh == 0 else o4[32:33, 512:1024]
                    sch.add("ACT", lambda e, sums_ap=sums_ap, hh=hh: e.activation(
                        sbrec2[:, hh, :], sums_ap, AF.Ln),
                        reads=[("ps", 5)], writes=[("recf", hh)], name="orec_ln")
            oq.append(oe_ln)
            for hh in range(2):
                def oe_bc(hh=hh):
                    obase = hh * 64
                    sch.add("ACT", lambda e, hh=hh: e.activation(
                        recb[:], sbrec2[:, hh, :], AF.Exp, scale=-1.0),
                        reads=[("recf", hh)], writes=[("recb",)], name="orec_exp")
                    bcap = o4[obase:obase + 64, 512:1024]
                    sch.add("PE", lambda e, bcap=bcap, obase=obase: e.matmul(
                        bcap, onesr64_sb[:], recb[:], start=True, stop=True,
                        tile_position=(0, obase)),
                        reads=[("recb",), ("onesr64",)], writes=[("ps", 5)], name="obc")
                oq.append(oe_bc)
                def oe_div(hh=hh, hp=hp):
                    obase = hh * 64
                    o_ap = o4[obase:obase + 64, 0:512]
                    bcap = o4[obase:obase + 64, 512:1024]
                    sch.add("DVE", lambda e, bcap=bcap, obase=obase: e.tensor_copy(
                        sbrec[obase:obase + 64, :], bcap),
                        reads=[("ps", 5)], writes=[("sbrec",)], name="obcc")
                    sch.add("DVE", lambda e, o_ap=o_ap, hp_g=hp, obase=obase: e.tensor_tensor(
                        OT[obase:obase + 64, hp_g, :], o_ap, sbrec[obase:obase + 64, :], ALU.mult),
                            reads=[("ps", 4), ("sbrec",)],
                            writes=[("OT", hp, obase)], name="odiv")
                oq.append(oe_div)
        opops(10 ** 9)
        if b == 0 and "x2e" in dbg:
            x2e_d = nc.dram_tensor("dump_x2e", [512, D], F32, kind="ExternalOutput").ap()
            sch.add("POOL", lambda e, x2e_d=x2e_d: e.dma_start(out=x2e_d, in_=x2_dram[0:512, :]),
                    reads=[("x2_dram", gt) for gt in range(4)],
                    writes=[("dump", "x2e")], dma=True, name="x2edma")
        if "block" in barriers:
            barrier(f"blk{b}")
        for t in (range(4) if ('proj' in parts and 'oevac' in parts and 'mm2' in parts) else ()):
            use_pb = "projb" in parts
            for dc in range(8):
                for half in range(2):
                    sch.add("PE", lambda e, t=t, dc=dc, half=half, use_pb=use_pb: e.matmul(
                        o4[:, 1024 + half * 512:1536 + half * 512], OT[:, dc, t * 128:(t + 1) * 128],
                        pwT[:, dc, half * 512:(half + 1) * 512],
                        start=(dc == 0), stop=(dc == 7 and not use_pb)),
                        reads=[("OT", dc, 0), ("OT", dc, 64), ("pwT", dc), ("wl_fence",)],
                        writes=[("ps", 6 + half)], name="proj")
            for half in (range(2) if use_pb else ()):
                sch.add("PE", lambda e, half=half: e.matmul(
                    o4[:, 1024 + half * 512:1536 + half * 512], onesr128b_sb[:], pb_sb[:, half * 512:(half + 1) * 512],
                    start=False, stop=True),
                    reads=[("pb",), ("onesr128b",), ("wl_fence",)], writes=[("ps", 6 + half)], name="projb")
            if "ln2" not in parts:
                continue
            pk = [("ps", 6), ("ps", 7)]
            o2ap = o4[:, 1024:2048]
            if t == 0 and b == 0 and "prj" in dbg:
                prj_d = nc.dram_tensor("dump_prj", [128, D], F32, kind="ExternalOutput").ap()
                sch.add("DVE", lambda e: e.tensor_copy(x2o[:, 1, :], o2ap),
                        reads=pk, writes=[("x2o", 1), ("dbgprj",)], name="prjcap")
                sch.add("POOL", lambda e, prj_d=prj_d: e.dma_start(out=prj_d, in_=x2o[:, 1, :]),
                        reads=[("dbgprj",), ("x2o", 1)], writes=[("dump", "prj")],
                        dma=True, name="prjdma")
            sch.add("DVE", lambda e, o2ap=o2ap: e.reduce_sum(
                nsum[:, 0:1], o2ap, axis=AX.X, negate=True),
                reads=pk, writes=[("nsum",)], name="ln2ns")
            sch.add("ACT", lambda e, o2ap=o2ap: e.activation(
                devnull[:], o2ap, AF.Square, accum_out=sumsq[:, 0:1]),
                reads=pk, writes=[("sumsq",), ("devnull",)], name="ln2sq")
            ln_chunk_rstd(NEWTON["ln2"], nt=1)
            xo = t % 2
            sch.add("DVE", lambda e, o2ap=o2ap: e.tensor_scalar(
                afft[:], o2ap, rst[:, 0:1], lbias[:, 0:1], ALU.mult, ALU.add),
                reads=pk + [("rstd",), ("lbias",)], writes=[("afft",)], name="ln2app_a")
            sch.add("DVE", lambda e, t=t, xo=xo: e.tensor_tensor(
                x2o[:, xo, :], afft[:], pool[:, t % 3, :], ALU.add),
                reads=[("afft",), ("pool", t % 3)],
                writes=[("x2o", xo)], name="ln2app_b")
            gt = b * 4 + t
            x2w = [("x2_dram", gt)]
            if b == NBLK - 1 and t == 3:
                x2w.append(("x2_fence",))
            def x2st(e=None, gt=gt, xo=xo, x2w=x2w):
                sch.add("SP", lambda e, gt=gt, xo=xo: e.dma_start(
                    out=x2_dram[gt * 128:(gt + 1) * 128, :], in_=x2o[:, xo, :]),
                    reads=[("x2o", xo)], writes=x2w, dma=True, name=f"x2st{gt}")
            if x2q:
                x2q.pop(0)()
            x2q.append(x2st)
            if t == 3:
                x2q.pop(0)()
            if t == 0:
                ld("SP", pool[:, 0, :], x_d[(ibase + 3 * 128):(ibase + 4 * 128), :],
                   f"xr{b * 4 + 3}", [("pool", 0)])
        if b == 0 and "x2e" in dbg:
            x2e_d = nc.dram_tensor("dump_x2e", [512, D], F32, kind="ExternalOutput").ap()
            sch.add("POOL", lambda e, x2e_d=x2e_d: e.dma_start(out=x2e_d, in_=x2_dram[0:512, :]),
                    reads=[("x2_dram", gt) for gt in range(4)],
                    writes=[("dump", "x2e")], dma=True, name="x2edma")
        if "block" in barriers:
            barrier(f"blk{b}")

    for c in range(NBLK if stage >= 3 else 0):
        for pair in range(2):
            for t in (2 * pair, 2 * pair + 1):
                gt = c * 4 + t
                ld("SP", pool[:, t % 3, :], x2_dram[gt * 128:(gt + 1) * 128, :],
                   f"x2l{gt}", [("pool", t % 3)], reads=[("x2_dram", gt), ("x2_fence",)])
            for t in (2 * pair, 2 * pair + 1):
                ln_stats_tile(pool[:, t % 3, :], t % 2, ("pool", t % 3))
            ln_chunk_rstd(NEWTON["ln3"], nt=2)
            for t in (2 * pair, 2 * pair + 1):
                tp = t % 2
                sch.add("DVE", lambda e, t=t, tp=tp: e.tensor_scalar(
                    xln[:, tp, :], pool[:, t % 3, :], rst[:, tp:tp + 1], lbias[:, tp:tp + 1], ALU.mult, ALU.add),
                    reads=[("pool", t % 3), ("rstd",), ("lbias",)], writes=[("xln", tp)], name="ln3app")
                for dc in range(8):
                    bpar = dc % 2
                    sch.add("PE", lambda e, tp=tp, dc=dc, bpar=bpar: e.transpose(
                        o4_bf[:, (2 + bpar) * 1024:(2 + bpar) * 1024 + 128],
                        xln[:, tp, dc * 128:(dc + 1) * 128], ident_sb[:]),
                        reads=[("xln", tp), ("ident",)], writes=[("ps", 6 + bpar)], name="x2lt")
                    sch.add("DVE", lambda e, t=t, dc=dc, bpar=bpar, c=c: e.tensor_copy(
                        XL[c % 2][:, dc, t * 128:(t + 1) * 128],
                        o4_bf[:, (2 + bpar) * 1024:(2 + bpar) * 1024 + 128]),
                        reads=[("ps", 6 + bpar)], writes=[("xlnT", c % 2, dc)], name="x2ltc")
        for rc in range(2):
            for dc in range(8):
                sch.add("PE", lambda e, rc=rc, dc=dc, c=c: e.matmul(
                    psA(2 + rc), u1[:, dc, rc * 128:(rc + 1) * 128], XL[c % 2][:, dc, :],
                    start=(dc == 0), stop=(dc == 7)),
                    reads=[("u1", dc), ("xlnT", c % 2, dc), ("wl_fence",)], writes=[("ps", 2 + rc)], name="u1mm")
            sch.add("DVE", lambda e, rc=rc: e.tensor_copy(ptb[:, rc, :], psA(2 + rc)),
                    reads=[("ps", 2 + rc)], writes=[("ptb", rc)], name="ptevac")
        def v1_gelu(fc):
            hb = fc % 2
            for rc in range(2):
                sch.add("PE", lambda e, fc=fc, rc=rc, hb=hb: e.matmul(
                    psO(hb), v1[:, rc, fc * 128:(fc + 1) * 128], ptb[:, rc, :],
                    start=(rc == 0), stop=(rc == 1)),
                    reads=[("v1", rc), ("ptb", rc), ("wl_fence",)], writes=[("ps", 4 + hb)], name="v1mm")
            sch.add("ACT", lambda e, fc=fc, hb=hb: e.activation(
                htb[:, hb, :], psO(hb), AF.Gelu_apprx_tanh, bias=b1T_sb[:, fc:fc + 1]),
                reads=[("ps", 4 + hb), ("b1T",)], writes=[("htb", hb)], name="gelu")

        def u2_mms(fc):
            hb = fc % 2
            for ts4 in range(4):
                q2ap = [sAB[:, 1024:1280], sAB[:, 1536:1792],
                        o4[:, 1024:1280], o4[:, 1536:1792]][ts4]
                q2bk = [2, 3, 6, 7][ts4]
                sch.add("PE", lambda e, fc=fc, hb=hb, ts4=ts4, q2ap=q2ap: e.matmul(
                    q2ap,
                    htb[:, hb, ts4 * 128:(ts4 + 1) * 128], u2[:, fc, :],
                    start=(fc == 0), stop=(fc == 31)),
                    reads=[("htb", hb), ("u2", fc), ("wl_fence",)], writes=[("ps", q2bk)], name="u2mm")

        for fc in range(33):
            if fc < 32:
                v1_gelu(fc)
            if fc >= 1:
                u2_mms(fc - 1)
        for ts4 in range(4):
            q2ap = [sAB[:, 1024:1280], sAB[:, 1536:1792],
                    o4[:, 1024:1280], o4[:, 1536:1792]][ts4]
            q2bk = [2, 3, 6, 7][ts4]
            sch.add("DVE", lambda e, ts4=ts4, q2ap=q2ap: e.tensor_copy(
                q2b[:, ts4, :], q2ap),
                reads=[("ps", q2bk)], writes=[("q2b", ts4)], name="q2evac")
        for ts4 in range(4):
            for rc in range(2):
                sch.add("PE", lambda e, ts4=ts4, rc=rc: e.transpose(
                    sAB_bf[:, (2 + rc) * 1024:(2 + rc) * 1024 + 128],
                    q2b[:, ts4, rc * 128:(rc + 1) * 128], ident_sb[:]),
                    reads=[("q2b", ts4), ("ident",)], writes=[("ps", 2 + rc)], name="q2t")
                sch.add("DVE", lambda e, ts4=ts4, rc=rc: e.tensor_copy(
                    q2t[:, rc, ts4, :], sAB_bf[:, (2 + rc) * 1024:(2 + rc) * 1024 + 128]),
                    reads=[("ps", 2 + rc)], writes=[("q2tk", rc, ts4)], name="q2tc")
        for t in range(4):
            gt = c * 4 + t
            ld("SP", pool[:, t % 3, :], x2_dram[gt * 128:(gt + 1) * 128, :],
               f"x2r{gt}", [("pool", t % 3)], reads=[("x2_dram", gt), ("x2_fence",)])
            for half in range(2):
                for rc in range(2):
                    sch.add("PE", lambda e, t=t, half=half, rc=rc: e.matmul(
                        psA(half), q2t[:, rc, t, :], v2[:, rc, half * 512:(half + 1) * 512],
                        start=(rc == 0), stop=False),
                        reads=[("q2tk", rc, t), ("v2", rc), ("wl_fence",)],
                        writes=[("ps", half)], name="v2mm")
                sch.add("PE", lambda e, half=half: e.matmul(
                    psA(half), onesr128b_sb[:], b2_sb[:, half * 512:(half + 1) * 512],
                    start=False, stop=True),
                    reads=[("b2",), ("onesr128b",), ("wl_fence",)], writes=[("ps", half)], name="v2b")
            pk = [("ps", 0), ("ps", 1)]
            o2ap = sAB[:, 0:1024]
            sch.add("DVE", lambda e, o2ap=o2ap: e.reduce_sum(
                nsum[:, 0:1], o2ap, axis=AX.X, negate=True),
                reads=pk, writes=[("nsum",)], name="ln4ns")
            sch.add("ACT", lambda e, o2ap=o2ap: e.activation(
                devnull[:], o2ap, AF.Square, accum_out=sumsq[:, 0:1]),
                reads=pk, writes=[("sumsq",), ("devnull",)], name="ln4sq")
            ln_chunk_rstd(NEWTON["ln4"], nt=1)
            xo = t % 2
            sch.add("DVE", lambda e, o2ap=o2ap: e.tensor_scalar(
                afft[:], o2ap, rst[:, 0:1], lbias[:, 0:1], ALU.mult, ALU.add),
                reads=pk + [("rstd",), ("lbias",)], writes=[("afft",)], name="ln4app_a")
            sch.add("DVE", lambda e, t=t, xo=xo: e.tensor_tensor(
                x2o[:, xo, :], afft[:], pool[:, t % 3, :], ALU.add),
                reads=[("afft",), ("pool", t % 3)],
                writes=[("x2o", xo)], name="ln4app_b")
            gt = c * 4 + t
            sch.add("POOL", lambda e, gt=gt, xo=xo: e.dma_start(
                out=out_d[gt * 128:(gt + 1) * 128, :], in_=x2o[:, xo, :]),
                reads=[("x2o", xo)], writes=[("outd", gt)], dma=True, name=f"outst{gt}")

    dump_specs = {
        "ptbd": (ptb[:], [128, 2, 512], BF16, [("ptb", rc) for rc in range(2)]),
        "htbd": (htb[:], [128, 2, 512], BF16, [("htb", hb) for hb in range(2)]),
        "q2bd": (q2b[:], [128, 4, 256], BF16, [("q2b", i) for i in range(4)]),
        "qres": (qres[:], [128, HP, S], BF16, [("qres", hp, c) for hp in range(8) for c in range(8)]),
        "negc_d": (negc[:], [128, 1], F32, [("negc",)]),
        "kd": (k_dram[:], [128, HP, NJT, 128], BF16,
               [("k_dram", c, hp) for c in range(8) for hp in range(8)]),
        "vd": (v_dram[:], [128, NJT, H, 65], BF16,
               [("v_dram", c, vo) for c in range(8) for vo in range(8)]),
        "x2d": (x2_dram[:], [S, D], F32, [("x2_dram", gt) for gt in range(32)]),
        "otd": (OT[:], [128, HP, 512], BF16,
                [("OT", hp, bb) for hp in range(8) for bb in (0, 64)]),
    }
    final_reads = [("outd", gt) for gt in range(32)] if stage >= 3 else []
    if stage == 2 and not ("proj" in parts and "ln2" in parts and "oevac" in parts and "mm2" in parts):
        final_reads = []
    for dn in dbg:
        if dn == "x2e":
            final_reads.append(("dump", "x2e"))
            continue
        if dn == "prj":
            final_reads.append(("dump", "prj"))
            continue
        ap, shp, dt, keys = dump_specs[dn]
        dd = nc.dram_tensor("dump_" + dn, shp, dt, kind="ExternalOutput").ap()
        sch.add("POOL", lambda e, dd=dd, ap=ap: e.dma_start(out=dd, in_=ap),
                reads=keys, writes=[("dump", dn)], dma=True, name="dump" + dn)
        final_reads.append(("dump", dn))
    if stage < 3 and not dbg:
        final_reads += [("k_dram", c, hp) for c in range(8) for hp in range(8)]

    sch.add("SP", lambda e: e.nop(), reads=final_reads, writes=[], name="final")

    sch.finalize()
    return sch


def kernel(L0_attn_x, L0_attn_qkv_w, L0_attn_qkv_b, L0_attn_proj_w, L0_attn_proj_b,
           L0_ffn_U1, L0_ffn_V1, L0_ffn_b1, L0_ffn_U2, L0_ffn_V2, L0_ffn_b2):
    x = np.ascontiguousarray(np.asarray(L0_attn_x, dtype=np.float32))
    consts = host_consts()
    common = {
        "wqT": np.ascontiguousarray(np.asarray(L0_attn_qkv_w, np.float32).T),
        "qkvbT": np.ascontiguousarray(np.asarray(L0_attn_qkv_b, np.float32).reshape(24, 128).T),
        "pwT": np.ascontiguousarray(np.asarray(L0_attn_proj_w, np.float32).T),
        "pb": np.asarray(L0_attn_proj_b, np.float32).reshape(1, D),
        "u1": np.ascontiguousarray(np.asarray(L0_ffn_U1, np.float32)),
        "v1": np.ascontiguousarray(np.asarray(L0_ffn_V1, np.float32)),
        "b1T": np.ascontiguousarray(np.asarray(L0_ffn_b1, np.float32).reshape(32, 128).T),
        "u2": np.ascontiguousarray(np.asarray(L0_ffn_U2, np.float32)),
        "v2": np.ascontiguousarray(np.asarray(L0_ffn_V2, np.float32)),
        "b2": np.asarray(L0_ffn_b2, np.float32).reshape(1, D),
        "cos_t": consts["cos_t"], "sin_t": consts["sin_t"],
        "rot": consts["rot"], "ident": consts["ident"], "masks": consts["masks"],
        "ones_c": consts["ones_c"], "ones_r64": consts["ones_r64"],
        "ones_r128f": consts["ones_r128f"], "ones_r128b": consts["ones_r128b"],
    }

    nc = bass.Bass("TRN2", target_bir_lowering=False, debug=False)
    es = ExitStack()
    with es:
        sch = build(nc, es)
        sems = {}
        for e in ENGINES:
            sems[("tick", e)] = es.enter_context(nc.semaphore(f"tick_{e}"))
            sems[("dma", e)] = es.enter_context(nc.semaphore(f"dma_{e}"))
        block = es.enter_context(nc.Block())
        block.sync(lambda e: sch.emit("SP", e, sems))
        block.tensor(lambda e: sch.emit("PE", e, sems))
        block.vector(lambda e: sch.emit("DVE", e, sems))
        block.scalar(lambda e: sch.emit("ACT", e, sems))
        block.gpsimd(lambda e: sch.emit("POOL", e, sems))

    in_maps = [dict(common, x=np.ascontiguousarray(x[i])) for i in range(8)]
    res = run_bass_kernel_spmd(nc, in_maps, core_ids=list(range(8)))
    out = np.stack([res.results[i]["out"] for i in range(8)], axis=0)
    return out.astype(np.float32)


# version 30 (final)
# speedup vs baseline: 1.2239x; 1.0312x over previous
import numpy as np
import ml_dtypes
import concourse.bass as bass
import concourse.mybir as mybir
from concourse.bass_utils import run_bass_kernel_spmd
from contextlib import ExitStack
from collections import deque

F32 = mybir.dt.float32
BF16 = mybir.dt.bfloat16
AF = mybir.ActivationFunctionType
ALU = mybir.AluOpType
AX = mybir.AxisListType

S, D, H, HD, HP = 4096, 1024, 16, 64, 8
DF, RK = 4096, 256
NBLK, CB, NJT = 8, 512, 32
EPS = 1e-5
SCALE = HD ** -0.5
NEWTON = {"ln1": 9, "ln2": 11, "ln3": 9, "ln4": 11}

ENGINES = ("PE", "ACT", "DVE", "POOL", "SP")


class Sched:

    def __init__(self):
        self.ops = []
        self.last_write = {}
        self.readers = {}

    def add(self, engine, fn, reads=(), writes=(), dma=False, name=""):
        idx = len(self.ops)
        waits = set()
        for k in reads:
            w = self.last_write.get(k)
            if w is not None:
                wo = self.ops[w]
                if wo["dma"] or wo["engine"] != engine or engine != "PE":
                    waits.add(w)
            if isinstance(k, tuple) and k and k[0] == "ps":
                for r in self.readers.get(k, ()):
                    if self.ops[r]["engine"] != engine:
                        waits.add(r)
        for k in writes:
            w = self.last_write.get(k)
            if w is not None:
                wo = self.ops[w]
                if wo["dma"] or dma or wo["engine"] != engine or engine != "PE":
                    waits.add(w)
            for r in self.readers.get(k, ()):
                ro = self.ops[r]
                if ro["dma"] or dma or ro["engine"] != engine or engine != "PE":
                    waits.add(r)
        for k in reads:
            self.readers.setdefault(k, []).append(idx)
        for k in writes:
            self.last_write[k] = idx
            self.readers[k] = []
        self.ops.append(dict(engine=engine, fn=fn, waits=sorted(waits),
                             dma=dma, name=name, needs_event=False, event=None))
        return idx

    def finalize(self):
        for op in self.ops:
            for w in op["waits"]:
                self.ops[w]["needs_event"] = True
        tick = {e: 0 for e in ENGINES}
        dmac = {e: 0 for e in ENGINES}
        for op in self.ops:
            if op["dma"]:
                dmac[op["engine"]] += 1
                op["event"] = ("dma", op["engine"], dmac[op["engine"]] * 16)
            elif op["needs_event"]:
                tick[op["engine"]] += 1
                op["event"] = ("tick", op["engine"], tick[op["engine"]])
        self.dma_totals = dmac

    def emit(self, engine, eng, sems):
        hw = {}
        for op in self.ops:
            if op["engine"] != engine:
                continue
            for w in op["waits"]:
                ev = self.ops[w]["event"]
                kind, e2, val = ev
                k2 = (kind, e2)
                if hw.get(k2, 0) >= val:
                    continue
                eng.wait_ge(sems[k2], val)
                hw[k2] = val
            inst = op["fn"](eng)
            ev = op["event"]
            if ev is not None:
                kind, e2, val = ev
                if kind == "dma":
                    inst.then_inc(sems[("dma", e2)], 16)
                else:
                    inst.then_inc(sems[("tick", e2)], 1)


def host_consts():
    pos = np.arange(S, dtype=np.float64)
    invf = 10000.0 ** (-(np.arange(0, HD, 2, dtype=np.float64) / HD))
    a = (np.arange(128) % 64) // 2
    ang = pos[None, :] * invf[a][:, None]
    cos_t = np.cos(ang).astype(ml_dtypes.bfloat16)
    sin_t = np.sin(ang).astype(ml_dtypes.bfloat16)
    rot = np.zeros((128, 128), np.float32)
    for i in range(0, 128, 2):
        rot[i + 1, i] = -1.0
        rot[i, i + 1] = 1.0
    rot = rot.astype(ml_dtypes.bfloat16)
    ident = np.eye(128, dtype=ml_dtypes.bfloat16)
    masks = np.zeros((128, 4, 512), np.float32)
    for o in range(4):
        for j in range(128):
            masks[j, o, o * 128 + j:] = 1.0
    masks = masks.astype(ml_dtypes.bfloat16)
    ones_c = np.ones((128, 1), dtype=ml_dtypes.bfloat16)
    ones_r64 = np.ones((1, 64), dtype=ml_dtypes.bfloat16)
    ones_r128f = np.ones((1, 128), np.float32)
    ones_r128b = np.ones((1, 128), dtype=ml_dtypes.bfloat16)
    return dict(cos_t=cos_t, sin_t=sin_t, rot=rot, ident=ident, masks=masks,
                ones_c=ones_c, ones_r64=ones_r64, ones_r128f=ones_r128f,
                ones_r128b=ones_r128b)


def build(nc, es, stage=3, dbg=(), nattn=NBLK, parts=frozenset({'mm1','exp','mask','mm2','oevac','proj','projb','ln2'}), barriers=()):
    def din(name, shape, dt=F32):
        return nc.dram_tensor(name, list(shape), dt, kind="ExternalInput").ap()

    x_d = din("x", (S, D))
    wqT_d = din("wqT", (D, 3 * D))
    qkvbT_d = din("qkvbT", (128, 24))
    pwT_d = din("pwT", (D, D))
    pb_d = din("pb", (1, D))
    u1_d = din("u1", (D, RK))
    v1_d = din("v1", (RK, DF))
    b1T_d = din("b1T", (128, 32))
    u2_d = din("u2", (DF, RK))
    v2_d = din("v2", (RK, D))
    b2_d = din("b2", (1, D))
    cos_d = din("cos_t", (128, S), BF16)
    sin_d = din("sin_t", (128, S), BF16)
    rot_d = din("rot", (128, 128), BF16)
    ident_d = din("ident", (128, 128), BF16)
    masks_d = din("masks", (128, 4, 512), BF16)
    onesc_d = din("ones_c", (128, 1), BF16)
    onesr64_d = din("ones_r64", (1, 64), BF16)
    onesr128f_d = din("ones_r128f", (1, 128), F32)
    onesr128b_d = din("ones_r128b", (1, 128), BF16)

    out_d = nc.dram_tensor("out", [S, D], F32, kind="ExternalOutput").ap()

    k_dram = nc.dram_tensor("k_scr", [128, HP, NJT, 128], BF16, kind="Internal").ap()
    v_dram = nc.dram_tensor("v_scr", [128, NJT, H, 65], BF16, kind="Internal").ap()
    x2_dram = nc.dram_tensor("x2_scr", [S, D], F32, kind="Internal").ap()

    cur = [16512]
    total_budget = 229376

    def sbuf(name, shape, dt, at=None):
        nbytes = int(np.prod(shape[1:])) * (2 if dt == BF16 else 4)
        nbytes = (nbytes + 31) // 32 * 32
        if at is None:
            off = cur[0]
            cur[0] += nbytes
        else:
            off = at
        assert off + nbytes <= total_budget, f"sbuf overflow at {name}: {off+nbytes}"
        return nc.alloc_sbuf_tensor_at(name, list(shape), dt, offset=off).ap()

    qres = sbuf("qres", (128, HP, S), BF16)
    OT = sbuf("OT", (128, HP, 512), BF16)
    pb0_off = cur[0]
    pbuf0 = sbuf("pbuf0", (128, 2048), BF16)
    pbuf1 = sbuf("pbuf1", (128, 1024), BF16)
    xlnTB = sbuf("xlnTB", (128, 8, 512), BF16, at=pb0_off)
    pmb = sbuf("pmb", (128, 2, 1024), BF16)
    kj = sbuf("kj", (128, 3, 4, 128), BF16)
    vj = sbuf("vj", (128, 3, 4, 2, 65), BF16)
    pool = sbuf("pool", (128, 3, D), F32)
    x2o = sbuf("x2o", (128, 2, D), F32)
    sbrec = sbuf("sbrec", (128, 512), BF16)
    cosb = sbuf("cosb", (128, 1, 512), BF16)
    sinb = sbuf("sinb", (128, 1, 512), BF16)
    masks_sb = sbuf("masks_sb", (128, 4, 512), BF16)
    rot_sb = sbuf("rot_sb", (128, 128), BF16)
    ident_sb = sbuf("ident_sb", (128, 128), BF16)
    onesc_sb = sbuf("onesc_sb", (128, 1), BF16)
    onesr64_sb = sbuf("onesr64_sb", (1, 64), BF16)
    onesr128f_sb = sbuf("onesr128f_sb", (1, 128), F32)
    onesr128b_sb = sbuf("onesr128b_sb", (1, 128), BF16)
    qkvbT_sb = sbuf("qkvbT_sb", (128, 24), F32)
    b1T_sb = sbuf("b1T_sb", (128, 32), F32)
    pb_sb = sbuf("pb_sb", (1, D), BF16)
    b2_sb = sbuf("b2_sb", (1, D), BF16)
    negc = sbuf("negc", (128, 1), F32)
    recb = sbuf("recb", (1, 512), BF16)
    nsum = sbuf("nsum", (128, 4), F32)
    sumsq = sbuf("sumsq", (128, 4), F32)
    msq = sbuf("msq", (128, 4), F32)
    veps = sbuf("veps", (128, 4), F32)
    rst = sbuf("rst", (128, 4), F32)
    lbias = sbuf("lbias", (128, 4), F32)
    rs1 = sbuf("rs1", (128, 4), F32)
    rs2 = sbuf("rs2", (128, 4), F32)
    qmax = sbuf("qmax", (1, 64), F32)
    kmax = sbuf("kmax", (1, 64), F32)
    cval = sbuf("cval", (1, 4), F32)
    devnull = sbuf("devnull", (128, 1024), BF16)
    xln = sbuf("xln", (128, 2, D), BF16)
    xlnT = sbuf("xlnT", (128, 8, 512), BF16)
    afft = sbuf("afft", (128, D), F32)

    ov_base = cur[0]
    wqT = sbuf("wqT_sb", (128, 8, 3 * D), BF16, at=ov_base)
    qkf = sbuf("qkf", (128, 2, 512), BF16, at=ov_base + 49152)
    tsin = sbuf("tsin", (128, 512), BF16, at=ov_base + 51200)
    tcos = sbuf("tcos", (128, 512), BF16, at=ov_base + 52224)
    sqb = sbuf("sqb", (128, 2, 512), BF16, at=ov_base + 53248)
    kstage = sbuf("kstage", (128, 2, 512), BF16, at=ov_base + 55296)
    vstage = sbuf("vstage", (128, 2, 4, 2, 65), BF16, at=ov_base + 57344)
    p1_end = ov_base + 57344 + 2 * 4 * 2 * 65 * 2
    pwT = sbuf("pwT_sb", (128, 8, D), BF16, at=ov_base)
    u1 = sbuf("u1_sb", (128, 8, RK), BF16, at=ov_base + 16384)
    v1 = sbuf("v1_sb", (128, 2, DF), BF16, at=ov_base + 20480)
    u2 = sbuf("u2_sb", (128, 32, RK), BF16, at=ov_base + 36864)
    v2 = sbuf("v2_sb", (128, 2, D), BF16, at=ov_base + 53248)
    ptb = sbuf("ptb", (128, 2, 512), BF16, at=ov_base + 57344)
    htb = sbuf("htb", (128, 2, 512), BF16, at=ov_base + 59392)
    q2b = sbuf("q2b", (128, 4, 256), BF16, at=ov_base + 61440)
    q2t = sbuf("q2t", (128, 2, 4, 128), BF16, at=ov_base + 63488)
    a_end = ov_base + 63488 + 2048
    sbrec2 = sbuf("sbrec2", (1, 2, 512), F32, at=ov_base + 59424)
    cur[0] = max(p1_end, a_end)
    assert cur[0] <= total_budget, f"total sbuf {cur[0]}"

    sAB = es.enter_context(nc.psum_tensor("sAB", [128, 2048], F32)).ap()
    o4 = es.enter_context(nc.psum_tensor("o4", [128, 2048], F32)).ap()

    def psA(bank):
        return sAB[:, bank * 512:(bank + 1) * 512]

    def psO(bank):
        return o4[:, bank * 512:(bank + 1) * 512]

    sAB_bf = sAB.bitcast(BF16)
    o4_bf = o4.bitcast(BF16)

    XL = [xlnT, xlnTB]

    sch = Sched()
    OV = "OVP1"
    barn = [0]

    def barrier(tag):
        barn[0] += 1
        bid = barn[0]
        for e in ENGINES:
            sch.add(e, lambda x: x.nop(), reads=(), writes=[("bar1", bid, e)],
                    name=f"bar1_{tag}_{e}")
        for e in ENGINES:
            sch.add(e, lambda x: x.nop(),
                    reads=[("bar1", bid, e2) for e2 in ENGINES],
                    writes=(), name=f"bar2_{tag}_{e}")

    def lp(f):
        def g(e):
            with nc.allow_low_precision(reason="softmax 1/sum broadcast in bf16; "
                                        "same rounding class as bf16 matmul operands"):
                return f(e)
        return g

    def ld(engine, dst, src, name, writes, reads=()):
        sch.add(engine, lambda e, d=dst, s=src: e.dma_start(out=d, in_=s),
                reads=reads, writes=writes, dma=True, name=name)

    ld("SP", masks_sb[:], masks_d[:], "masks", [("masks",)])
    ld("SP", rot_sb[:], rot_d[:], "rot", [("rot",)])
    ld("SP", ident_sb[:], ident_d[:], "ident", [("ident",)])
    ld("SP", onesc_sb[:], onesc_d[:], "onesc", [("onesc",)])
    ld("SP", onesr64_sb[:], onesr64_d[:], "onesr64", [("onesr64",)])
    ld("SP", onesr128f_sb[:], onesr128f_d[:], "onesr128f", [("onesr128f",)])
    ld("SP", onesr128b_sb[:], onesr128b_d[:], "onesr128b", [("onesr128b",)])
    ld("SP", qkvbT_sb[:], qkvbT_d[:], "qkvbT", [("qkvbT",)])
    ld("SP", b1T_sb[:], b1T_d[:], "b1T", [("b1T",)])
    for dc in range(8):
        wk = [("wqT", dc), OV] + ([("wq_fence",)] if dc == 7 else [])
        ld("POOL", wqT[:, dc, :], wqT_d[dc * 128:(dc + 1) * 128, :],
           f"wqT{dc}", wk)
    sch.add("DVE", lambda e: e.memset(pbuf0[:], 0.0), reads=(), writes=[("pbz0",)], name="pbz0")
    sch.add("DVE", lambda e: e.memset(pbuf1[:], 0.0), reads=[("pbz0",)], writes=[("pb", 0), ("pb", 1), ("pb", 2)], name="pbz1")
    sch.add("DVE", lambda e: e.memset(vstage[:, :, :, :, 64:65], 1.0),
            reads=[OV], writes=[("vones",)], name="vones")

    def newton_rstd(src_key, n_iter, wslice):
        V = veps[:, wslice]
        sch.add("ACT", lambda e: e.activation(rs1[:, wslice], V, AF.Ln),
                reads=[src_key], writes=[("rs1",)], name="rstd_ln")
        sch.add("ACT", lambda e: e.activation(rst[:, wslice], rs1[:, wslice], AF.Exp, scale=-0.5),
                reads=[("rs1",)], writes=[("rstd",)], name="rstd_exp")

    def ln_stats_tile(xap, tslot, xkey):
        sch.add("DVE", lambda e: e.reduce_sum(nsum[:, tslot:tslot + 1], xap, axis=AX.X, negate=True),
                reads=[xkey], writes=[("nsum",)], name="ln_ns")
        sch.add("ACT", lambda e: e.activation(devnull[:], xap, AF.Square,
                                              accum_out=sumsq[:, tslot:tslot + 1]),
                reads=[xkey], writes=[("sumsq",), ("devnull",)], name="ln_sq")

    def ln_chunk_rstd(n_iter, nt=4):
        sl = slice(0, nt)
        sch.add("DVE", lambda e: e.tensor_single_scalar(msq[:, sl], nsum[:, sl], 1.0 / D, ALU.mult),
                reads=[("nsum",)], writes=[("msq",)])
        sch.add("DVE", lambda e: e.tensor_tensor(rs1[:, sl], msq[:, sl], msq[:, sl], ALU.mult),
                reads=[("msq",)], writes=[("rs1",)])
        sch.add("DVE", lambda e: e.scalar_tensor_tensor(veps[:, sl], sumsq[:, sl], 1.0 / D,
                                                        rs1[:, sl], ALU.mult, ALU.subtract),
                reads=[("sumsq",), ("rs1",)], writes=[("veps",)])
        sch.add("DVE", lambda e: e.tensor_single_scalar(veps[:, sl], veps[:, sl], EPS, ALU.add),
                reads=[("veps",)], writes=[("veps",)])
        newton_rstd(("veps",), n_iter, sl)
        sch.add("DVE", lambda e: e.tensor_tensor(lbias[:, sl], msq[:, sl], rst[:, sl], ALU.mult),
                reads=[("msq",), ("rstd",)], writes=[("lbias",)])

    for c in range(NBLK):
        cs = slice(c * CB, (c + 1) * CB)
        cpar = 0
        ld("SP", cosb[:, cpar, :], cos_d[:, cs], f"cos{c}", [("cosb", cpar), OV])
        ld("SP", sinb[:, cpar, :], sin_d[:, cs], f"sin{c}", [("sinb", cpar), OV])
        for pair in range(2):
            for t in (2 * pair, 2 * pair + 1):
                gt = c * 4 + t
                ld("SP", pool[:, t % 3, :], x_d[gt * 128:(gt + 1) * 128, :], f"x{gt}", [("pool", t % 3)])
            for t in (2 * pair, 2 * pair + 1):
                ln_stats_tile(pool[:, t % 3, :], t % 2, ("pool", t % 3))
            ln_chunk_rstd(NEWTON["ln1"], nt=2)
            for t in (2 * pair, 2 * pair + 1):
                tp = t % 2
                sch.add("DVE", lambda e, t=t, tp=tp: e.tensor_scalar(
                    xln[:, tp, :], pool[:, t % 3, :], rst[:, tp:tp + 1], lbias[:, tp:tp + 1], ALU.mult, ALU.add),
                    reads=[("pool", t % 3), ("rstd",), ("lbias",)], writes=[("xln", tp)], name="ln1app")
                for dc in range(8):
                    bpar = dc % 2
                    sch.add("PE", lambda e, tp=tp, dc=dc, bpar=bpar: e.transpose(
                        o4_bf[:, (2 + bpar) * 1024:(2 + bpar) * 1024 + 128],
                        xln[:, tp, dc * 128:(dc + 1) * 128], ident_sb[:]),
                        reads=[("xln", tp), ("ident",)], writes=[("ps", 6 + bpar)], name="xlt")
                    sch.add("DVE", lambda e, t=t, dc=dc, bpar=bpar, c=c: e.tensor_copy(
                        XL[c % 2][:, dc, t * 128:(t + 1) * 128],
                        o4_bf[:, (2 + bpar) * 1024:(2 + bpar) * 1024 + 128]),
                        reads=[("ps", 6 + bpar)], writes=[("xlnT", c % 2, dc)], name="xltc")
        def stageA(oc):
            bk = oc % 2
            for dc in range(8):
                sch.add("PE", lambda e, oc=oc, dc=dc, bk=bk, c=c: e.matmul(
                    psA(bk), wqT[:, dc, oc * 128:(oc + 1) * 128], XL[c % 2][:, dc, :],
                    start=(dc == 0), stop=(dc == 7)),
                    reads=[("wqT", dc), ("wq_fence",), ("xlnT", c % 2, dc), OV], writes=[("ps", bk)], name="qkv")

        def stageB(oc, c, cpar, cs):
            bk = oc % 2
            if oc < 16:
                qk = oc % 2
                sch.add("ACT", lambda e, oc=oc, bk=bk, qk=qk: e.activation(
                    qkf[:, qk, :], psA(bk), AF.Identity, bias=qkvbT_sb[:, oc:oc + 1]),
                    reads=[("ps", bk), ("qkvbT",)], writes=[("qkf", qk)], name="qkevac")
                sch.add("DVE", lambda e, qk=qk, cpar=cpar: e.tensor_tensor(
                    tsin[:], qkf[:, qk, :], sinb[:, cpar, :], ALU.mult),
                    reads=[("qkf", qk), ("sinb", cpar), OV], writes=[("tsin",)], name="tsin")
                sch.add("DVE", lambda e, qk=qk, cpar=cpar: e.tensor_tensor(
                    tcos[:], qkf[:, qk, :], cosb[:, cpar, :], ALU.mult),
                    reads=[("qkf", qk), ("cosb", cpar), OV], writes=[("tcos",)], name="tcos")
                rb = 2 + (oc % 2)
                sch.add("PE", lambda e, rb=rb: e.matmul(psA(rb), rot_sb[:], tsin[:],
                                                        start=True, stop=False),
                        reads=[("rot",), ("tsin",)], writes=[("ps", rb)], name="ropemm1")
                sch.add("PE", lambda e, rb=rb: e.matmul(psA(rb), ident_sb[:], tcos[:],
                                                        start=False, stop=True),
                        reads=[("ident",), ("tcos",)], writes=[("ps", rb)], name="ropemm2")
                if oc < 8:
                    dst, dkey = qres[:, oc, cs], ("qres", oc, c)
                else:
                    ksl = (oc - 8) % 2
                    dst, dkey = kstage[:, ksl, :], ("kstage", ksl)
                sch.add("ACT", lambda e, rb=rb, dst=dst: e.activation(dst, psA(rb), AF.Copy),
                        reads=[("ps", rb), OV], writes=[dkey], name="ropeevac")
                sq = oc % 2
                sch.add("DVE", lambda e, dst=dst, sq=sq: e.tensor_tensor(sqb[:, sq, :], dst, dst, ALU.mult),
                        reads=[dkey], writes=[("sqb", sq)], name="normsq")
                if oc >= 8:
                    ksl = (oc - 8) % 2
                    sch.add("POOL", lambda e, c=c, oc=oc, ksl=ksl: e.dma_start(
                        out=k_dram[:, oc - 8, c * 4:(c + 1) * 4, :],
                        in_=kstage[:, ksl, :].rearrange("p (t f) -> p t f", t=4)),
                        reads=[("kstage", ksl), OV], writes=[("k_dram", c, oc - 8)],
                        dma=True, name=f"kdma{c}_{oc-8}")
            else:
                vo = oc - 16
                sch.add("ACT", lambda e, oc=oc, bk=bk: e.activation(
                    qkf[:, 0, :], psA(bk), AF.Identity, bias=qkvbT_sb[:, oc:oc + 1]),
                    reads=[("ps", bk), ("qkvbT",)], writes=[("qkf", 0)], name="vevac")
                vsl = vo % 2
                for tsub in range(4):
                    vb = tsub % 2
                    sch.add("PE", lambda e, tsub=tsub, vb=vb: e.transpose(
                        o4_bf[:, vb * 1024:vb * 1024 + 128],
                        qkf[:, 0, tsub * 128:(tsub + 1) * 128], ident_sb[:]),
                        reads=[("qkf", 0), ("ident",)], writes=[("ps", 4 + vb)], name="vt")
                    for hh in range(2):
                        sch.add("DVE", lambda e, tsub=tsub, vb=vb, vsl=vsl, hh=hh: e.tensor_copy(
                            vstage[:, vsl, tsub, hh, 0:64],
                            o4_bf[:, vb * 1024 + hh * 64:vb * 1024 + hh * 64 + 64]),
                            reads=[("ps", 4 + vb), ("vones",)],
                            writes=[("vstage", vsl)], name="vc")
                vdw = [("v_dram", c, vo)]
                if c == NBLK - 1 and vo == 7:
                    vdw.append(("kv_fence",))
                sch.add("POOL", lambda e, c=c, vo=vo, vsl=vsl: e.dma_start(
                    out=v_dram[:, c * 4:(c + 1) * 4, 2 * vo:2 * vo + 2, :],
                    in_=vstage[:, vsl, :, :, :]),
                    reads=[("vstage", vsl), OV], writes=vdw,
                    dma=True, name=f"vdma{c}_{vo}")

        def stageC(oc, c):
            sq = oc % 2
            sch.add("PE", lambda e, sq=sq: e.matmul(o4[0:1, 0:512], onesc_sb[:], sqb[:, sq, :],
                                                    start=True, stop=True),
                    reads=[("onesc",), ("sqb", sq)], writes=[("ps", 4)], name="normmm")
            mx, mslot = (qmax, c * 8 + oc) if oc < 8 else (kmax, c * 8 + oc - 8)
            sch.add("DVE", lambda e, mx=mx, mslot=mslot: e.reduce_max(
                mx[:, mslot:mslot + 1], o4[0:1, 0:512], axis=AX.X),
                reads=[("ps", 4)], writes=[("max", mslot, mx is qmax)], name="normmax")

        for oc in range(26):
            if oc < 24:
                stageA(oc)
            if 1 <= oc < 25:
                stageB(oc - 1, c, cpar, cs)
            if oc >= 2 and oc - 2 < 16:
                stageC(oc - 2, c)
    sch.add("DVE", lambda e: e.reduce_max(cval[:, 0:1], qmax[:], axis=AX.X),
            reads=[("max", i, True) for i in range(64)], writes=[("cval",)], name="cq")
    sch.add("DVE", lambda e: e.reduce_max(cval[:, 1:2], kmax[:], axis=AX.X),
            reads=[("max", i, False) for i in range(64)] + [("cval",)], writes=[("cval",)], name="ck")
    sch.add("DVE", lambda e: e.tensor_tensor(veps[0:1, 0:1], cval[:, 0:1], cval[:, 1:2], ALU.mult),
            reads=[("cval",)], writes=[("veps",)], name="cmul")
    newton_rstd(("veps",), 12, slice(0, 1))
    sch.add("DVE", lambda e: e.tensor_tensor(cval[:, 2:3], veps[0:1, 0:1], rst[0:1, 0:1], ALU.mult),
            reads=[("veps",), ("rstd",)], writes=[("cval",)], name="csqrt")
    sch.add("DVE", lambda e: e.tensor_single_scalar(cval[:, 3:4], cval[:, 2:3], -SCALE, ALU.mult),
            reads=[("cval",)], writes=[("cneg",)], name="cns")
    sch.add("PE", lambda e: e.matmul(o4[:, 0:1], onesr128f_sb[:], cval[0:1, 3:4],
                                     start=True, stop=True),
            reads=[("cneg",), ("onesr128f",)], writes=[("ps", 4)], name="cbc")
    sch.add("ACT", lambda e: e.activation(negc[:], o4[:, 0:1], AF.Copy),
            reads=[("ps", 4)], writes=[("negc",)], name="cbcc")

    for dc in range(8):
        ld("POOL", pwT[:, dc, :], pwT_d[dc * 128:(dc + 1) * 128, :], f"pwT{dc}",
           [("pwT", dc), OV])
        ld("POOL", u1[:, dc, :], u1_d[dc * 128:(dc + 1) * 128, :], f"u1{dc}",
           [("u1", dc), OV])
    for rc in range(2):
        ld("POOL", v1[:, rc, :], v1_d[rc * 128:(rc + 1) * 128, :], f"v1{rc}",
           [("v1", rc), OV])
        ld("POOL", v2[:, rc, :], v2_d[rc * 128:(rc + 1) * 128, :], f"v2{rc}",
           [("v2", rc), OV])
    for fc in range(32):
        ld("POOL", u2[:, fc, :], u2_d[fc * 128:(fc + 1) * 128, :], f"u2{fc}",
           [("u2", fc), OV])
    ld("POOL", pb_sb[:], pb_d[:], "pb", [("pb",), OV])
    ld("POOL", b2_sb[:], b2_d[:], "b2", [("b2",), OV, ("wl_fence",)])

    pbring = lambda i: [pbuf0[:, 0:1024], pbuf0[:, 1024:2048],
                        pbuf1[:, 0:1024]][i % 3]
    x2q = []
    pq = deque()
    oq = deque()

    def ppops(n):
        for _ in range(n):
            if not pq:
                return
            pq.popleft()()

    def push_proj(b):
        for t in range(4):
            def bundleA(t=t, b=b):
                gt = b * 4 + t
                ld("SP", pool[:, t % 3, :], x_d[gt * 128:(gt + 1) * 128, :],
                   f"xr{gt}", [("pool", t % 3)])
                for dc in range(4):
                    for half in range(2):
                        sch.add("PE", lambda e, t=t, dc=dc, half=half: e.matmul(
                            o4[:, 1024 + half * 512:1536 + half * 512], OT[:, dc, t * 128:(t + 1) * 128],
                            pwT[:, dc, half * 512:(half + 1) * 512],
                            start=(dc == 0), stop=False),
                            reads=[("OT", dc, 0), ("OT", dc, 64), ("pwT", dc), ("wl_fence",)],
                            writes=[("ps", 6 + half)], name="proj")
            pq.append(bundleA)
            def bundleB(t=t):
                for dc in range(4, 8):
                    for half in range(2):
                        sch.add("PE", lambda e, t=t, dc=dc, half=half: e.matmul(
                            o4[:, 1024 + half * 512:1536 + half * 512], OT[:, dc, t * 128:(t + 1) * 128],
                            pwT[:, dc, half * 512:(half + 1) * 512],
                            start=False, stop=False),
                            reads=[("OT", dc, 0), ("OT", dc, 64), ("pwT", dc), ("wl_fence",)],
                            writes=[("ps", 6 + half)], name="proj")
                for half in range(2):
                    sch.add("PE", lambda e, half=half: e.matmul(
                        o4[:, 1024 + half * 512:1536 + half * 512], onesr128b_sb[:], pb_sb[:, half * 512:(half + 1) * 512],
                        start=False, stop=True),
                        reads=[("pb",), ("onesr128b",), ("wl_fence",)], writes=[("ps", 6 + half)], name="projb")
            pq.append(bundleB)
            def bundleC(t=t, b=b):
                pk = [("ps", 6), ("ps", 7)]
                o2ap = o4[:, 1024:2048]
                sch.add("DVE", lambda e, o2ap=o2ap: e.reduce_sum(
                    nsum[:, 0:1], o2ap, axis=AX.X, negate=True),
                    reads=pk, writes=[("nsum",)], name="ln2ns")
                sch.add("ACT", lambda e, o2ap=o2ap: e.activation(
                    devnull[:], o2ap, AF.Square, accum_out=sumsq[:, 0:1]),
                    reads=pk, writes=[("sumsq",), ("devnull",)], name="ln2sq")
                ln_chunk_rstd(NEWTON["ln2"], nt=1)
                xo = t % 2
                sch.add("DVE", lambda e, o2ap=o2ap: e.tensor_scalar(
                    afft[:], o2ap, rst[:, 0:1], lbias[:, 0:1], ALU.mult, ALU.add),
                    reads=pk + [("rstd",), ("lbias",)], writes=[("afft",)], name="ln2app_a")
                sch.add("DVE", lambda e, t=t, xo=xo: e.tensor_tensor(
                    x2o[:, xo, :], afft[:], pool[:, t % 3, :], ALU.add),
                    reads=[("afft",), ("pool", t % 3)],
                    writes=[("x2o", xo)], name="ln2app_b")
                gt = b * 4 + t
                x2w = [("x2_dram", gt)]
                if b == NBLK - 1 and t == 3:
                    x2w.append(("x2_fence",))
                def x2st(gt=gt, xo=xo, x2w=x2w):
                    sch.add("SP", lambda e, gt=gt, xo=xo: e.dma_start(
                        out=x2_dram[gt * 128:(gt + 1) * 128, :], in_=x2o[:, xo, :]),
                        reads=[("x2o", xo)], writes=x2w, dma=True, name=f"x2st{gt}")
                if x2q:
                    x2q.pop(0)()
                x2q.append(x2st)
                if t == 3 and b == NBLK - 1:
                    x2q.pop(0)()
            pq.append(bundleC)

    def opops(n):
        for _ in range(n):
            if not oq:
                return
            oq.popleft()()
    for b in range(NBLK if stage >= 2 else 0):
        if b >= 1:
            push_proj(b - 1)
        ibase = b * CB
        njt = 4 * (b + 1)
        njg = b + 1
        for hp in range(8):
            h0 = 2 * hp

            def kvload(g):
                gs = g % 3
                ld("SP", kj[:, gs, :, :],
                   k_dram[:, hp, g * 4:(g + 1) * 4, :],
                   f"kj{b}_{hp}_{g}", [("kj", gs)],
                   reads=[("k_dram", g, hp), ("kv_fence",), ("wl_fence",)])
                ld("SP", vj[:, gs, :, :, :],
                   v_dram[:, g * 4:(g + 1) * 4, h0:h0 + 2, :],
                   f"vj{b}_{hp}_{g}", [("vj", gs)],
                   reads=[("v_dram", g, hp), ("kv_fence",), ("wl_fence",)])

            kvload(0)
            if "mm1" not in parts:
                continue
            next_g = [1]
            for j in range(njt + 2):
                opops(2)
                ppops(1)
                while next_g[0] < njg and 4 * next_g[0] - 6 <= j:
                    kvload(next_g[0])
                    next_g[0] += 1
                if j < njt:
                    jslot = (j // 4) % 3
                    jj = j % 4
                    sset = j % 2
                    sv = [sAB[:, 0:1024], sAB[:, 1024:2048]][sset]
                    sk = [(0, 1), (2, 3)][sset]
                    for hh in range(2):
                        base = hh * 64
                        sch.add("PE", lambda e, jslot=jslot, jj=jj, hh=hh, base=base,
                                sv=sv, hp=hp, ibase=ibase: e.matmul(
                            sv[:, hh * 512:(hh + 1) * 512],
                            kj[base:base + 64, jslot, jj, :],
                            qres[base:base + 64, hp, ibase:ibase + 512],
                            start=True, stop=True,
                            tile_position=(base, 0)),
                            reads=[("kj", jslot), ("qres", hp, b)],
                            writes=[("ps", sk[hh])], name="mm1")
                je = j - 1
                if 0 <= je < njt and "exp" in parts:
                    sset = je % 2
                    sv = [sAB[:, 0:1024], sAB[:, 1024:2048]][sset]
                    sk = [(0, 1), (2, 3)][sset]
                    pslot = je % 3
                    off = max(0, (je - 4 * b) * 128)
                    pb_ap = pbring(pslot).rearrange("p (h f) -> p h f", h=2)[:, :, off:]
                    sv_ap = sv.rearrange("p (h f) -> p h f", h=2)[:, :, off:]
                    sch.add("ACT", lambda e, sv_ap=sv_ap, pb_ap=pb_ap: e.activation(
                        pb_ap, sv_ap, AF.Exp, bias=negc[:], scale=SCALE),
                        reads=[("ps", sk[0]), ("ps", sk[1]), ("negc",)],
                        writes=[("pb", pslot)], name="exp")
                    if je >= 4 * b and "mask" in parts:
                        o = je - 4 * b
                        ms = je % 2
                        for hh in range(2):
                            sch.add("DVE", lambda e, pslot=pslot, hh=hh, o=o, ms=ms: e.tensor_tensor(
                                pmb[:, ms, hh * 512:(hh + 1) * 512],
                                pbring(pslot)[:, hh * 512:(hh + 1) * 512],
                                masks_sb[:, o, :], ALU.mult),
                                reads=[("pb", pslot), ("masks",)],
                                writes=[("pm", ms, hh)], name="mask")
                jm = j - 2
                if 0 <= jm < njt and "mm2" in parts:
                    jslot = (jm // 4) % 3
                    jj = jm % 4
                    pslot = jm % 3
                    diag = (jm >= 4 * b) and ("mask" in parts)
                    ms = jm % 2
                    srcs, skeys = [], []
                    for hh in range(2):
                        if diag:
                            srcs.append(pmb[:, ms, hh * 512:(hh + 1) * 512])
                            skeys.append([("pm", ms, hh)])
                        else:
                            srcs.append(pbring(pslot)[:, hh * 512:(hh + 1) * 512])
                            skeys.append([("pb", pslot)])
                    sch.add("PE", lambda e, jslot=jslot, jj=jj, jm=jm, njt=njt, src=srcs[0]: e.matmul(
                        o4[0:64, 0:512],
                        vj[:, jslot, jj, 0, 0:64], src,
                        start=(jm == 0), stop=(jm == njt - 1),
                        tile_position=(0, 0)),
                        reads=[("vj", jslot)] + skeys[0],
                        writes=[("ps", 4)], name="mm2e")
                    sch.add("PE", lambda e, jslot=jslot, jj=jj, jm=jm, njt=njt, src=srcs[1]: e.matmul(
                        o4[64:128, 0:512],
                        vj[:, jslot, jj, 1, 0:64], src,
                        start=(jm == 0), stop=(jm == njt - 1),
                        tile_position=(0, 64)),
                        reads=[("vj", jslot)] + skeys[1],
                        writes=[("ps", 4)], name="mm2o")
                    sch.add("PE", lambda e, jm=jm, njt=njt, src=srcs[0]: e.matmul(
                        o4[0:1, 512:1024],
                        onesc_sb[:], src,
                        start=(jm == 0), stop=(jm == njt - 1),
                        tile_position=(0, 0)),
                        reads=[("onesc",)] + skeys[0],
                        writes=[("ps", 5)], name="mm2s")
                    sch.add("PE", lambda e, jm=jm, njt=njt, src=srcs[1]: e.matmul(
                        o4[32:33, 512:1024],
                        onesc_sb[:], src,
                        start=(jm == 0), stop=(jm == njt - 1),
                        tile_position=(0, 32)),
                        reads=[("onesc",)] + skeys[1],
                        writes=[("ps", 5)], name="mm2s")
            def oe_ln(hp=hp):
                for hh in range(2):
                    sums_ap = o4[0:1, 512:1024] if hh == 0 else o4[32:33, 512:1024]
                    sch.add("ACT", lambda e, sums_ap=sums_ap, hh=hh: e.activation(
                        sbrec2[:, hh, :], sums_ap, AF.Ln),
                        reads=[("ps", 5)], writes=[("recf", hh)], name="orec_ln")
            oq.append(oe_ln)
            for hh in range(2):
                def oe_bc(hh=hh):
                    obase = hh * 64
                    sch.add("ACT", lambda e, hh=hh: e.activation(
                        recb[:], sbrec2[:, hh, :], AF.Exp, scale=-1.0),
                        reads=[("recf", hh)], writes=[("recb",)], name="orec_exp")
                    bcap = o4[obase:obase + 64, 512:1024]
                    sch.add("PE", lambda e, bcap=bcap, obase=obase: e.matmul(
                        bcap, onesr64_sb[:], recb[:], start=True, stop=True,
                        tile_position=(0, obase)),
                        reads=[("recb",), ("onesr64",)], writes=[("ps", 5)], name="obc")
                oq.append(oe_bc)
                def oe_div(hh=hh, hp=hp):
                    obase = hh * 64
                    o_ap = o4[obase:obase + 64, 0:512]
                    bcap = o4[obase:obase + 64, 512:1024]
                    sch.add("DVE", lambda e, bcap=bcap, obase=obase: e.tensor_copy(
                        sbrec[obase:obase + 64, :], bcap),
                        reads=[("ps", 5)], writes=[("sbrec",)], name="obcc")
                    sch.add("DVE", lambda e, o_ap=o_ap, hp_g=hp, obase=obase: e.tensor_tensor(
                        OT[obase:obase + 64, hp_g, :], o_ap, sbrec[obase:obase + 64, :], ALU.mult),
                            reads=[("ps", 4), ("sbrec",)],
                            writes=[("OT", hp, obase)], name="odiv")
                oq.append(oe_div)
        opops(10 ** 9)
        if b == 0 and "x2e" in dbg:
            x2e_d = nc.dram_tensor("dump_x2e", [512, D], F32, kind="ExternalOutput").ap()
            sch.add("POOL", lambda e, x2e_d=x2e_d: e.dma_start(out=x2e_d, in_=x2_dram[0:512, :]),
                    reads=[("x2_dram", gt) for gt in range(4)],
                    writes=[("dump", "x2e")], dma=True, name="x2edma")
        if "block" in barriers:
            barrier(f"blk{b}")
        ppops(10 ** 9)
        if b == 0 and "x2e" in dbg:
            x2e_d = nc.dram_tensor("dump_x2e", [512, D], F32, kind="ExternalOutput").ap()
            sch.add("POOL", lambda e, x2e_d=x2e_d: e.dma_start(out=x2e_d, in_=x2_dram[0:512, :]),
                    reads=[("x2_dram", gt) for gt in range(4)],
                    writes=[("dump", "x2e")], dma=True, name="x2edma")
        if "block" in barriers:
            barrier(f"blk{b}")

    push_proj(NBLK - 1)
    ppops(10 ** 9)

    for c in range(NBLK if stage >= 3 else 0):
        for pair in range(2):
            for t in (2 * pair, 2 * pair + 1):
                gt = c * 4 + t
                ld("SP", pool[:, t % 3, :], x2_dram[gt * 128:(gt + 1) * 128, :],
                   f"x2l{gt}", [("pool", t % 3)], reads=[("x2_dram", gt), ("x2_fence",)])
            for t in (2 * pair, 2 * pair + 1):
                ln_stats_tile(pool[:, t % 3, :], t % 2, ("pool", t % 3))
            ln_chunk_rstd(NEWTON["ln3"], nt=2)
            for t in (2 * pair, 2 * pair + 1):
                tp = t % 2
                sch.add("DVE", lambda e, t=t, tp=tp: e.tensor_scalar(
                    xln[:, tp, :], pool[:, t % 3, :], rst[:, tp:tp + 1], lbias[:, tp:tp + 1], ALU.mult, ALU.add),
                    reads=[("pool", t % 3), ("rstd",), ("lbias",)], writes=[("xln", tp)], name="ln3app")
                for dc in range(8):
                    bpar = dc % 2
                    sch.add("PE", lambda e, tp=tp, dc=dc, bpar=bpar: e.transpose(
                        o4_bf[:, (2 + bpar) * 1024:(2 + bpar) * 1024 + 128],
                        xln[:, tp, dc * 128:(dc + 1) * 128], ident_sb[:]),
                        reads=[("xln", tp), ("ident",)], writes=[("ps", 6 + bpar)], name="x2lt")
                    sch.add("DVE", lambda e, t=t, dc=dc, bpar=bpar, c=c: e.tensor_copy(
                        XL[c % 2][:, dc, t * 128:(t + 1) * 128],
                        o4_bf[:, (2 + bpar) * 1024:(2 + bpar) * 1024 + 128]),
                        reads=[("ps", 6 + bpar)], writes=[("xlnT", c % 2, dc)], name="x2ltc")
        for rc in range(2):
            for dc in range(8):
                sch.add("PE", lambda e, rc=rc, dc=dc, c=c: e.matmul(
                    psA(2 + rc), u1[:, dc, rc * 128:(rc + 1) * 128], XL[c % 2][:, dc, :],
                    start=(dc == 0), stop=(dc == 7)),
                    reads=[("u1", dc), ("xlnT", c % 2, dc), ("wl_fence",)], writes=[("ps", 2 + rc)], name="u1mm")
            sch.add("DVE", lambda e, rc=rc: e.tensor_copy(ptb[:, rc, :], psA(2 + rc)),
                    reads=[("ps", 2 + rc)], writes=[("ptb", rc)], name="ptevac")
        def v1_gelu(fc):
            hb = fc % 2
            for rc in range(2):
                sch.add("PE", lambda e, fc=fc, rc=rc, hb=hb: e.matmul(
                    psO(hb), v1[:, rc, fc * 128:(fc + 1) * 128], ptb[:, rc, :],
                    start=(rc == 0), stop=(rc == 1)),
                    reads=[("v1", rc), ("ptb", rc), ("wl_fence",)], writes=[("ps", 4 + hb)], name="v1mm")
            sch.add("ACT", lambda e, fc=fc, hb=hb: e.activation(
                htb[:, hb, :], psO(hb), AF.Gelu_apprx_tanh, bias=b1T_sb[:, fc:fc + 1]),
                reads=[("ps", 4 + hb), ("b1T",)], writes=[("htb", hb)], name="gelu")

        def u2_mms(fc):
            hb = fc % 2
            for ts4 in range(4):
                q2ap = [sAB[:, 1024:1280], sAB[:, 1536:1792],
                        o4[:, 1024:1280], o4[:, 1536:1792]][ts4]
                q2bk = [2, 3, 6, 7][ts4]
                sch.add("PE", lambda e, fc=fc, hb=hb, ts4=ts4, q2ap=q2ap: e.matmul(
                    q2ap,
                    htb[:, hb, ts4 * 128:(ts4 + 1) * 128], u2[:, fc, :],
                    start=(fc == 0), stop=(fc == 31)),
                    reads=[("htb", hb), ("u2", fc), ("wl_fence",)], writes=[("ps", q2bk)], name="u2mm")

        for fc in range(33):
            if fc < 32:
                v1_gelu(fc)
            if fc >= 1:
                u2_mms(fc - 1)
        for ts4 in range(4):
            q2ap = [sAB[:, 1024:1280], sAB[:, 1536:1792],
                    o4[:, 1024:1280], o4[:, 1536:1792]][ts4]
            q2bk = [2, 3, 6, 7][ts4]
            sch.add("DVE", lambda e, ts4=ts4, q2ap=q2ap: e.tensor_copy(
                q2b[:, ts4, :], q2ap),
                reads=[("ps", q2bk)], writes=[("q2b", ts4)], name="q2evac")
        for ts4 in range(4):
            for rc in range(2):
                sch.add("PE", lambda e, ts4=ts4, rc=rc: e.transpose(
                    sAB_bf[:, (2 + rc) * 1024:(2 + rc) * 1024 + 128],
                    q2b[:, ts4, rc * 128:(rc + 1) * 128], ident_sb[:]),
                    reads=[("q2b", ts4), ("ident",)], writes=[("ps", 2 + rc)], name="q2t")
                sch.add("DVE", lambda e, ts4=ts4, rc=rc: e.tensor_copy(
                    q2t[:, rc, ts4, :], sAB_bf[:, (2 + rc) * 1024:(2 + rc) * 1024 + 128]),
                    reads=[("ps", 2 + rc)], writes=[("q2tk", rc, ts4)], name="q2tc")
        for t in range(4):
            gt = c * 4 + t
            ld("SP", pool[:, t % 3, :], x2_dram[gt * 128:(gt + 1) * 128, :],
               f"x2r{gt}", [("pool", t % 3)], reads=[("x2_dram", gt), ("x2_fence",)])
            for half in range(2):
                for rc in range(2):
                    sch.add("PE", lambda e, t=t, half=half, rc=rc: e.matmul(
                        psA(half), q2t[:, rc, t, :], v2[:, rc, half * 512:(half + 1) * 512],
                        start=(rc == 0), stop=False),
                        reads=[("q2tk", rc, t), ("v2", rc), ("wl_fence",)],
                        writes=[("ps", half)], name="v2mm")
                sch.add("PE", lambda e, half=half: e.matmul(
                    psA(half), onesr128b_sb[:], b2_sb[:, half * 512:(half + 1) * 512],
                    start=False, stop=True),
                    reads=[("b2",), ("onesr128b",), ("wl_fence",)], writes=[("ps", half)], name="v2b")
            pk = [("ps", 0), ("ps", 1)]
            o2ap = sAB[:, 0:1024]
            sch.add("DVE", lambda e, o2ap=o2ap: e.reduce_sum(
                nsum[:, 0:1], o2ap, axis=AX.X, negate=True),
                reads=pk, writes=[("nsum",)], name="ln4ns")
            sch.add("ACT", lambda e, o2ap=o2ap: e.activation(
                devnull[:], o2ap, AF.Square, accum_out=sumsq[:, 0:1]),
                reads=pk, writes=[("sumsq",), ("devnull",)], name="ln4sq")
            ln_chunk_rstd(NEWTON["ln4"], nt=1)
            xo = t % 2
            sch.add("DVE", lambda e, o2ap=o2ap: e.tensor_scalar(
                afft[:], o2ap, rst[:, 0:1], lbias[:, 0:1], ALU.mult, ALU.add),
                reads=pk + [("rstd",), ("lbias",)], writes=[("afft",)], name="ln4app_a")
            sch.add("DVE", lambda e, t=t, xo=xo: e.tensor_tensor(
                x2o[:, xo, :], afft[:], pool[:, t % 3, :], ALU.add),
                reads=[("afft",), ("pool", t % 3)],
                writes=[("x2o", xo)], name="ln4app_b")
            gt = c * 4 + t
            sch.add("POOL", lambda e, gt=gt, xo=xo: e.dma_start(
                out=out_d[gt * 128:(gt + 1) * 128, :], in_=x2o[:, xo, :]),
                reads=[("x2o", xo)], writes=[("outd", gt)], dma=True, name=f"outst{gt}")

    dump_specs = {
        "ptbd": (ptb[:], [128, 2, 512], BF16, [("ptb", rc) for rc in range(2)]),
        "htbd": (htb[:], [128, 2, 512], BF16, [("htb", hb) for hb in range(2)]),
        "q2bd": (q2b[:], [128, 4, 256], BF16, [("q2b", i) for i in range(4)]),
        "qres": (qres[:], [128, HP, S], BF16, [("qres", hp, c) for hp in range(8) for c in range(8)]),
        "negc_d": (negc[:], [128, 1], F32, [("negc",)]),
        "kd": (k_dram[:], [128, HP, NJT, 128], BF16,
               [("k_dram", c, hp) for c in range(8) for hp in range(8)]),
        "vd": (v_dram[:], [128, NJT, H, 65], BF16,
               [("v_dram", c, vo) for c in range(8) for vo in range(8)]),
        "x2d": (x2_dram[:], [S, D], F32, [("x2_dram", gt) for gt in range(32)]),
        "otd": (OT[:], [128, HP, 512], BF16,
                [("OT", hp, bb) for hp in range(8) for bb in (0, 64)]),
    }
    final_reads = [("outd", gt) for gt in range(32)] if stage >= 3 else []
    if stage == 2 and not ("proj" in parts and "ln2" in parts and "oevac" in parts and "mm2" in parts):
        final_reads = []
    for dn in dbg:
        if dn == "x2e":
            final_reads.append(("dump", "x2e"))
            continue
        if dn == "prj":
            final_reads.append(("dump", "prj"))
            continue
        ap, shp, dt, keys = dump_specs[dn]
        dd = nc.dram_tensor("dump_" + dn, shp, dt, kind="ExternalOutput").ap()
        sch.add("POOL", lambda e, dd=dd, ap=ap: e.dma_start(out=dd, in_=ap),
                reads=keys, writes=[("dump", dn)], dma=True, name="dump" + dn)
        final_reads.append(("dump", dn))
    if stage < 3 and not dbg:
        final_reads += [("k_dram", c, hp) for c in range(8) for hp in range(8)]

    sch.add("SP", lambda e: e.nop(), reads=final_reads, writes=[], name="final")

    sch.finalize()
    return sch


def kernel(L0_attn_x, L0_attn_qkv_w, L0_attn_qkv_b, L0_attn_proj_w, L0_attn_proj_b,
           L0_ffn_U1, L0_ffn_V1, L0_ffn_b1, L0_ffn_U2, L0_ffn_V2, L0_ffn_b2):
    x = np.ascontiguousarray(np.asarray(L0_attn_x, dtype=np.float32))
    consts = host_consts()
    common = {
        "wqT": np.ascontiguousarray(np.asarray(L0_attn_qkv_w, np.float32).T),
        "qkvbT": np.ascontiguousarray(np.asarray(L0_attn_qkv_b, np.float32).reshape(24, 128).T),
        "pwT": np.ascontiguousarray(np.asarray(L0_attn_proj_w, np.float32).T),
        "pb": np.asarray(L0_attn_proj_b, np.float32).reshape(1, D),
        "u1": np.ascontiguousarray(np.asarray(L0_ffn_U1, np.float32)),
        "v1": np.ascontiguousarray(np.asarray(L0_ffn_V1, np.float32)),
        "b1T": np.ascontiguousarray(np.asarray(L0_ffn_b1, np.float32).reshape(32, 128).T),
        "u2": np.ascontiguousarray(np.asarray(L0_ffn_U2, np.float32)),
        "v2": np.ascontiguousarray(np.asarray(L0_ffn_V2, np.float32)),
        "b2": np.asarray(L0_ffn_b2, np.float32).reshape(1, D),
        "cos_t": consts["cos_t"], "sin_t": consts["sin_t"],
        "rot": consts["rot"], "ident": consts["ident"], "masks": consts["masks"],
        "ones_c": consts["ones_c"], "ones_r64": consts["ones_r64"],
        "ones_r128f": consts["ones_r128f"], "ones_r128b": consts["ones_r128b"],
    }

    nc = bass.Bass("TRN2", target_bir_lowering=False, debug=False)
    es = ExitStack()
    with es:
        sch = build(nc, es)
        sems = {}
        for e in ENGINES:
            sems[("tick", e)] = es.enter_context(nc.semaphore(f"tick_{e}"))
            sems[("dma", e)] = es.enter_context(nc.semaphore(f"dma_{e}"))
        block = es.enter_context(nc.Block())
        block.sync(lambda e: sch.emit("SP", e, sems))
        block.tensor(lambda e: sch.emit("PE", e, sems))
        block.vector(lambda e: sch.emit("DVE", e, sems))
        block.scalar(lambda e: sch.emit("ACT", e, sems))
        block.gpsimd(lambda e: sch.emit("POOL", e, sems))

    in_maps = [dict(common, x=np.ascontiguousarray(x[i])) for i in range(8)]
    res = run_bass_kernel_spmd(nc, in_maps, core_ids=list(range(8)))
    out = np.stack([res.results[i]["out"] for i in range(8)], axis=0)
    return out.astype(np.float32)
